# version 17
# baseline (speedup 1.0000x reference)
import contextlib
import numpy as np
import ml_dtypes
import concourse.bass as bass
import concourse.mybir as mybir
from concourse.bass_utils import run_bass_kernel_spmd
from concourse.alu_op_type import AluOpType

F32 = mybir.dt.float32
BF16 = mybir.dt.bfloat16
F16 = mybir.dt.float16
AF = mybir.ActivationFunctionType
AX = mybir.AxisListType
OP = AluOpType

NCORES = 8
BPC = 4
T = 2048
TC = 512
NCH = T // TC
DM = 1024
KM = DM // 128
DI = 2048
KI = DI // 128
DS = 16
NE = 8
H = 2048
MH = 32

ENGS = ("sync", "tensor", "vector", "scalar", "gpsimd")


class Buf:
    __slots__ = ("writer", "readers")

    def __init__(self):
        self.writer = None
        self.readers = []


class Sched:
    def __init__(self):
        self.ops = []
        self.last_dma = {}

    def op(self, eng, fn, reads=(), writes=(), dma=None, tag=""):
        i = len(self.ops)
        deps = set()
        if dma is not None and dma in self.last_dma:
            deps.add(self.last_dma[dma])
        for b in reads:
            if b.writer is not None:
                deps.add(b.writer)
        for b in writes:
            if b.writer is not None:
                deps.add(b.writer)
            deps.update(b.readers)
        deps.discard(i)
        self.ops.append([i, eng, fn, deps, dma, tag])
        if dma is not None:
            self.last_dma[dma] = i
        for b in reads:
            b.readers.append(i)
        for b in writes:
            b.writer = i
            b.readers = []
        return i

    def build_streams(self):
        ops = self.ops
        n = len(ops)
        has_cross = [False] * n
        for i, eng, fn, deps, dma, tag in ops:
            for d in deps:
                if ops[d][1] != eng or eng != "tensor":
                    has_cross[d] = True
        ecnt = {e: 0 for e in ENGS}
        dcnt = {}
        tok = [None] * n
        for i, eng, fn, deps, dma, tag in ops:
            if dma is not None:
                dcnt[dma] = dcnt.get(dma, 0) + 16
                tok[i] = ("d", dma, dcnt[dma])
            elif has_cross[i]:
                ecnt[eng] += 1
                tok[i] = ("e", eng, ecnt[eng])
        streams = {e: [] for e in ENGS}
        water = {e: {} for e in ENGS}
        drained_upto = {e: -1 for e in ENGS}
        last_own = {e: -1 for e in ENGS}
        run_dcnt = {}
        for i, eng, fn, deps, dma, tag in ops:
            st = streams[eng]
            need_drain = False
            for d in sorted(deps):
                deng, ddma = ops[d][1], ops[d][4]
                if ddma is not None:
                    val = run_dcnt.get(ddma, 0)
                    assert val >= tok[d][2], f"dma dep {d} after waiter {i}?"
                    key = ("d", ddma)
                    if water[eng].get(key, -1) < val:
                        water[eng][key] = val
                        st.append(("wait", key, val))
                elif deng != eng or eng != "tensor":
                    tk = tok[d]
                    assert tk is not None, f"dep {d} ({ops[d][5]}) of {i} ({tag}) untokenized"
                    key = (tk[0], tk[1])
                    if water[eng].get(key, -1) < tk[2]:
                        water[eng][key] = tk[2]
                        st.append(("wait", key, tk[2]))
                else:
                    pass
            if need_drain:
                st.append(("drain",))
                drained_upto[eng] = last_own[eng]
            st.append(("op", i))
            if dma is not None:
                run_dcnt[dma] = run_dcnt.get(dma, 0) + 16
            else:
                last_own[eng] = i
        return streams, tok, dcnt, ecnt


def build_nc():
    nc = bass.Bass("TRN2", target_bir_lowering=False)

    d_x = nc.dram_tensor("x", [BPC, T, DM], F32, kind="ExternalInput").ap()
    d_win = nc.dram_tensor("w_in", [DM, 2 * DI], F32, kind="ExternalInput").ap()
    d_wdl = nc.dram_tensor("w_delta", [DI, DI], F32, kind="ExternalInput").ap()
    d_bdl = nc.dram_tensor("b_delta", [DI], F32, kind="ExternalInput").ap()
    d_wb = nc.dram_tensor("w_b", [DI, DS], F32, kind="ExternalInput").ap()
    d_wc = nc.dram_tensor("w_c", [DI, DS], F32, kind="ExternalInput").ap()
    d_alog = nc.dram_tensor("a_log", [DI, DS], F32, kind="ExternalInput").ap()
    d_dpar = nc.dram_tensor("d_param", [DI], F32, kind="ExternalInput").ap()
    d_wout = nc.dram_tensor("w_out", [DI, DM], F32, kind="ExternalInput").ap()
    d_wr = nc.dram_tensor("wr", [NE, DM], F32, kind="ExternalInput").ap()
    d_br = nc.dram_tensor("br", [NE], F32, kind="ExternalInput").ap()
    d_wl1 = nc.dram_tensor("wl1", [NE, DM, 2 * H], F32, kind="ExternalInput").ap()
    d_bl1 = nc.dram_tensor("bl1", [NE, 2 * H], F32, kind="ExternalInput").ap()
    d_wl2 = nc.dram_tensor("wl2", [NE, H, DM], F32, kind="ExternalInput").ap()
    d_bl2 = nc.dram_tensor("bl2", [NE, DM], F32, kind="ExternalInput").ap()
    d_bn = nc.dram_tensor("biases_n", [NE], F32, kind="ExternalInput").ap()
    d_y = nc.dram_tensor("y", [BPC, T, DM], F32, kind="ExternalOutput").ap()

    s_win = nc.dram_tensor("s_win", [DM, 2 * DI], BF16).ap()
    s_wdl = nc.dram_tensor("s_wdl", [DI, DI], BF16).ap()
    s_wout = nc.dram_tensor("s_wout", [DI, DM], BF16).ap()
    s_wl1 = nc.dram_tensor("s_wl1", [NE, DM, 2 * H], BF16).ap()
    s_wl2 = nc.dram_tensor("s_wl2", [NE, H, DM], BF16).ap()
    s_x1 = nc.dram_tensor("s_x1", [BPC, T, DM], F32).ap()
    s_x1nt = nc.dram_tensor("s_x1nt", [BPC, DM, T], BF16).ap()

    es = contextlib.ExitStack()

    def sbuf(name, p, f, dt):
        return es.enter_context(nc.sbuf_tensor(name, [p, f], dt))

    def psum(name, p, f, dt):
        return es.enter_context(nc.psum_tensor(name, [p, f], dt))

    cpool = sbuf("cpool", 128, 8, F32)
    ident_b = sbuf("ident_b", 128, 128, BF16)
    ident_f = sbuf("ident_f", 128, 128, F32)
    onehot = sbuf("onehot", 128, 16 * 128, BF16)
    a_col = sbuf("a_col", 128, KI * DS, F32)
    d_colv = sbuf("d_colv", 128, KI, F32)
    bdl_col = sbuf("bdl_col", 128, KI, F32)
    bl1_col = sbuf("bl1_col", 128, NE * MH, F32)
    bl2_col = sbuf("bl2_col", 128, NE * KM, F32)
    br_row = sbuf("br_row", 1, NE, F32)
    bn_row = sbuf("bn_row", 1, NE, F32)
    ones_r = sbuf("ones_r", 1, 128, F32)
    br_bc = sbuf("br_bc", 128, NE, F32)
    bn_bc = sbuf("bn_bc", 128, NE, F32)
    wr_sb = sbuf("wr_sb", 128, KM * NE, BF16)
    wr_tmp = sbuf("wr_tmp", 8, DM, BF16)
    w_b_sb = sbuf("w_b_sb", 128, KI * 128, BF16)
    w_c_sb = sbuf("w_c_sb", 128, KI * 128, BF16)
    h_state = sbuf("h_state", 128, KI * DS, F32)

    mx_roll = sbuf("mx_roll", 128, 2048, F32)
    xn_t = sbuf("xn_t", 128, 1024, BF16)
    xnt = sbuf("xnt", 128, KM * TC, BF16)
    t_su = sbuf("t_su", 128, KI * TC, BF16)
    delta_g = sbuf("delta_g", 128, 4 * TC, F16)
    du_g = sbuf("du_g", 128, 4 * TC, F16)
    da_b = sbuf("da_b", 128, 8 * TC, F16)
    bin_b = sbuf("bin_b", 128, 4 * TC, F16)
    hseg_b = sbuf("hseg_b", 128, 4 * TC, F16)
    hp_b = sbuf("hp_b", 128, 8 * TC, F16)
    y_g = sbuf("y_g", 128, 4 * TC, F16)
    gat_t = sbuf("gat_t", 128, TC, F16)
    sp_a = sbuf("sp_a", 128, 4 * TC, F16)
    sres_t = sbuf("sres_t", 128, 2 * TC, BF16)
    bmt = sbuf("bmt", 128, TC, BF16)
    cmt = sbuf("cmt", 128, TC, BF16)
    x1n_t = sbuf("x1n_t", 128, 1024, BF16)
    x1nt = sbuf("x1nt", 128, KM * TC, BF16)
    ymix_t = sbuf("ymix_t", 128, KM * TC, BF16)
    lnst = sbuf("lnst", 128, 12, F32)
    lnmv = sbuf("lnmv", 128, 2, F32)
    lnr = sbuf("lnr", 128, 4, F32)
    fx_roll = sbuf("fx_roll", 128, 2048, F32)
    t_b2 = sbuf("t_b2", 128, KM * TC, BF16)
    t_g = sbuf("t_g", 128, KI * TC, BF16)
    y_acct = sbuf("y_acct", 128, KM * TC, F32)
    sa_b = sbuf("sa_b", 128, 2 * TC, BF16)
    sb_b2 = sbuf("sb_b2", 128, 2 * TC, BF16)
    wtmp = sbuf("wtmp", 128, 2 * TC, BF16)
    tmp_s = sbuf("tmp_s", 128, 2 * TC, F16)
    ewt_sb = sbuf("ewt_sb", 128, TC, BF16)
    ewb_sb = sbuf("ewb_sb", 128, TC, F16)
    sc_sb = sbuf("sc_sb", 128, 4 * NE, F32)
    ew_sb = sbuf("ew_sb", 128, 4 * NE, F32)
    tk_sb = sbuf("tk_sb", 128, 6 * NE, F32)
    tk1 = sbuf("tk1", 128, 8, F32)
    t_w = sbuf("t_w", 128, 4 * 4096, BF16)

    p_mm = [psum("p_mm0", 128, 512, F32), psum("p_mm1", 128, 512, F32)]
    p_ax = [psum("p_ax0", 128, 512, F32), psum("p_ax1", 128, 512, F32)]
    p_bb = psum("p_bb", 128, 512, F32)
    p_cf = psum("p_cf", 128, 512, F32)
    p_t32 = psum("p_t32", 128, 512, F32)
    p_t16 = psum("p_t16", 128, 1024, BF16)

    sems = {}
    for name in ("pe", "v", "s", "g",
                 "ld_x", "ld_b2", "ld_w0", "ld_w1", "ld_w2", "ld_w3",
                 "st_win", "st_wdl", "st_wout", "st_wl1", "st_wl2",
                 "st_x1", "st_x1nt", "st_y"):
        sems[name] = es.enter_context(nc.semaphore("sem_" + name))

    S = Sched()
    B = {}

    def buf(key):
        if key not in B:
            B[key] = Buf()
        return B[key]

    def E(eng, fn, reads=(), writes=(), dma=None, tag=""):
        return S.op(eng, fn, reads, writes, dma, tag)

    def mm(out_ap, lhs_ap, rhs_ap, start, stop, reads, wbuf, tag=""):
        E("tensor", lambda En, o=out_ap, l=lhs_ap, r=rhs_ap, a=start, z=stop:
          En.matmul(o, l, r, start=a, stop=z), reads, [wbuf], tag=tag)

    def tp(out_ap, in_ap, id_ap, reads, writes, tag=""):
        E("tensor", lambda En, o=out_ap, i=in_ap, d=id_ap: En.transpose(o, i, d),
          reads, writes, tag=tag)

    def tt(eng, alu, out_ap, a_ap, b_ap, reads, writes, tag=""):
        E(eng, lambda En, o=out_ap, a=a_ap, b=b_ap, u=alu: En.tensor_tensor(o, a, b, u),
          reads, writes, tag=tag)

    def ts(eng, out_ap, in_ap, s1, s2, op0, op1, reads, writes, tag=""):
        if op1 is None:
            E(eng, lambda En, o=out_ap, i=in_ap, a=s1, u=op0: En.tensor_scalar(o, i, a, None, u),
              reads, writes, tag=tag)
        else:
            E(eng, lambda En, o=out_ap, i=in_ap, a=s1, b=s2, u=op0, w=op1:
              En.tensor_scalar(o, i, a, b, u, w), reads, writes, tag=tag)

    def stt(out_ap, in0_ap, sc, in1_ap, op0, op1, reads, writes, tag="", eng="vector"):
        E(eng, lambda En, o=out_ap, i=in0_ap, s=sc, j=in1_ap, u=op0, w=op1:
          En.scalar_tensor_tensor(o, i, s, j, u, w), reads, writes, tag=tag)

    def act(out_ap, in_ap, func, reads, writes, bias=None, scale=None, tag=""):
        kw = {}
        if bias is not None:
            kw["bias"] = bias
        if scale is not None:
            kw["scale"] = scale
        E("scalar", lambda En, o=out_ap, i=in_ap, f=func, k=kw: En.activation(o, i, f, **k),
          reads, writes, tag=tag)

    def cp(eng, out_ap, in_ap, reads, writes, tag=""):
        E(eng, lambda En, o=out_ap, i=in_ap: En.tensor_copy(o, i), reads, writes, tag=tag)

    def red(kind, out_ap, in_ap, reads, writes, tag=""):
        if kind == "max":
            E("vector", lambda En, o=out_ap, i=in_ap: En.reduce_max(o, i, axis=AX.X),
              reads, writes, tag=tag)
        else:
            E("vector", lambda En, o=out_ap, i=in_ap: En.reduce_sum(o, i, axis=AX.X),
              reads, writes, tag=tag)

    def scan(out_ap, d0, d1, init_ap, reads, writes, tag=""):
        E("vector", lambda En, o=out_ap, a=d0, b=d1, i=init_ap:
          En.tensor_tensor_scan(o, a, b, i, OP.mult, OP.add), reads, writes, tag=tag)

    def dma(q, out_ap, in_ap, reads, writes, sem, tag=""):
        E(q, lambda En, o=out_ap, i=in_ap: En.dma_start(o, i), reads, writes, dma=sem, tag=tag)

    def mset(ap, val, writes, tag=""):
        E("gpsimd", lambda En, a=ap, v=val: En.memset(a, v), (), writes, tag=tag)

    nc.const_aps.register(F32, 1.0, cpool[:, 0:1])
    nc.const_aps.register(F32, -0.5, cpool[:, 1:2])
    nc.const_aps.register(F32, 1e-5, cpool[:, 2:3])
    nc.const_aps.register(F32, -1.0, cpool[:, 3:4])

    mxr_b = [buf(("mxr", i)) for i in range(2)]
    STG = mxr_b

    cp_b = buf("cpool")
    for j, val in enumerate([1.0, -0.5, 1e-5, -1.0]):
        mset(cpool[:, j:j + 1], val, [cp_b], tag="cpool")
    mset(ones_r[:, :], 1.0, [buf("ones")], tag="ones")
    idf_b, idb_b = buf("identf"), buf("identb")
    E("gpsimd", lambda En: En.iota(ident_f[:, :], [[1, 128]], channel_multiplier=-1,
                                   allow_small_or_imprecise_dtypes=True),
      (), [idf_b], tag="iota")
    ts("gpsimd", ident_f[:, :], ident_f[:, :], 0.0, None, OP.is_equal, None,
       [idf_b], [idf_b], tag="identf")
    cp("gpsimd", ident_b[:, :], ident_f[:, :], [idf_b], [idb_b], tag="identb")
    oh_b = buf("oh")
    E("gpsimd", lambda En: En.iota(mx_roll[0:128, 0:2048], [[1, 16], [0, 128]],
                                   channel_multiplier=-1,
                                   allow_small_or_imprecise_dtypes=True),
      (), STG, tag="ohiota")
    ts("gpsimd", onehot[:, :], mx_roll[0:128, 0:2048], 0.0, None, OP.is_equal, None,
       STG, [oh_b], tag="onehot")

    dma("sync", mx_roll[:, 0:KI * DS].rearrange("p (m n) -> p m n", m=KI),
        d_alog.rearrange("(m p) n -> p m n", p=128), STG, STG, "ld_x", tag="ld_alog")
    acol_b = buf("acol")
    act(a_col[:, :], mx_roll[:, 0:KI * DS], AF.Exp, STG, [acol_b], tag="a_exp")
    ts("vector", a_col[:, :], a_col[:, :], -1.0, None, OP.mult, None,
       [acol_b], [acol_b], tag="a_neg")
    pt32_b0 = buf("p_t32")
    idf_b0 = buf("identf")

    def vec_load(dst_ap, dst_key, src_ap, rows, tg):
        dma("sync", mx_roll[0:rows, 0:128], src_ap, STG, STG, "ld_x", tag=tg + "l")
        tp(p_t32[:, 0:rows], mx_roll[0:rows, 0:128], ident_f[0:rows, 0:rows],
           STG + [idf_b0], [pt32_b0], tag=tg + "t")
        cp("vector", dst_ap, p_t32[:, 0:rows], [pt32_b0], [buf(dst_key)], tag=tg + "e")

    vec_load(d_colv[:, :], "dcol", d_dpar.rearrange("(m p) -> m p", p=128), 16, "vdp")
    vec_load(bdl_col[:, :], "bdl", d_bdl.rearrange("(m p) -> m p", p=128), 16, "vbd")
    for e_ in range(NE):
        vec_load(bl1_col[:, e_ * MH:(e_ + 1) * MH], "bl1",
                 d_bl1[e_, :].rearrange("(m p) -> m p", p=128), 32, f"vb1{e_}")
        vec_load(bl2_col[:, e_ * KM:(e_ + 1) * KM], "bl2",
                 d_bl2[e_, :].rearrange("(m p) -> m p", p=128), 8, f"vb2{e_}")
    dma("sync", br_row[:, :], d_br.rearrange("(a n) -> a n", a=1), (), [buf("brrow")], "ld_x")
    dma("sync", bn_row[:, :], d_bn.rearrange("(a n) -> a n", a=1), (), [buf("bnrow")], "ld_x")
    mset(w_b_sb[:, :], 0.0, [buf("wbs")], tag="wbz")
    mset(w_c_sb[:, :], 0.0, [buf("wcs")], tag="wcz")
    mset(bmt[:, :], 0.0, [buf("bmt")], tag="bmz")
    mset(cmt[:, :], 0.0, [buf("cmt")], tag="cmz")
    mset(ewt_sb[:, :], 0.0, [buf("ewt")], tag="ewz")
    dma("sync", mx_roll[:, 0:KI * DS].rearrange("p (m n) -> p m n", m=KI),
        d_wb.rearrange("(m p) n -> p m n", p=128), STG, STG, "ld_x")
    cp("vector", w_b_sb[:, :].rearrange("p (k c) -> p k c", k=KI)[:, :, 0:DS],
       mx_roll[:, 0:KI * DS].rearrange("p (k c) -> p k c", k=KI), STG, [buf("wbs")], tag="wb_cast")
    dma("sync", mx_roll[:, 0:KI * DS].rearrange("p (m n) -> p m n", m=KI),
        d_wc.rearrange("(m p) n -> p m n", p=128), STG, STG, "ld_x")
    cp("vector", w_c_sb[:, :].rearrange("p (k c) -> p k c", k=KI)[:, :, 0:DS],
       mx_roll[:, 0:KI * DS].rearrange("p (k c) -> p k c", k=KI), STG, [buf("wcs")], tag="wc_cast")
    dma("sync", mx_roll[0:8, 0:DM], d_wr, STG, STG, "ld_x")
    cp("vector", wr_tmp[:, :], mx_roll[0:8, 0:DM], STG, [buf("wrb")], tag="wr_cast")

    for row_b, row, dst, dkey in ((buf("brrow"), br_row, br_bc, "brbc"),
                                  (buf("bnrow"), bn_row, bn_bc, "bnbc")):
        mm(p_ax[0][:, 0:NE], ones_r[0:1, :], row[0:1, :], True, True,
           [row_b, buf("ones")], buf("p_ax0"), tag="bc_" + dkey)
        cp("vector", dst[:, :], p_ax[0][:, 0:NE], [buf("p_ax0")], [buf(dkey)], tag="bce_" + dkey)

    for k in range(KM):
        tp(p_t16[:, 0:NE], wr_tmp[:, k * 128:(k + 1) * 128], ident_b[0:8, 0:8],
           [buf("wrb"), idb_b], [buf("p_t16h0")], tag=f"wr_tp{k}")
        cp("vector", wr_sb[:, k * NE:(k + 1) * NE], p_t16[:, 0:NE],
           [buf("p_t16h0")], [buf("wrT")], tag=f"wr_ev{k}")

    cast_jobs = []
    win_regs, wdl_regs, wout_regs = [], [], []
    wl1_regs = [[] for _ in range(NE)]
    wl2_regs = [[] for _ in range(NE)]
    for kt in range(KM):
        r = buf(("scr", "win", kt)); win_regs.append(r)
        for hh in range(2):
            cast_jobs.append((d_win[kt * 128:(kt + 1) * 128, hh * 2048:(hh + 1) * 2048],
                              s_win[kt * 128:(kt + 1) * 128, hh * 2048:(hh + 1) * 2048],
                              2048, r, "st_win"))
    for kt in range(KI):
        r = buf(("scr", "wdl", kt)); wdl_regs.append(r)
        cast_jobs.append((d_wdl[kt * 128:(kt + 1) * 128, :],
                          s_wdl[kt * 128:(kt + 1) * 128, :], 2048, r, "st_wdl"))
    for kt in range(KI):
        r = buf(("scr", "wout", kt)); wout_regs.append(r)
        cast_jobs.append((d_wout[kt * 128:(kt + 1) * 128, :],
                          s_wout[kt * 128:(kt + 1) * 128, :], 1024, r, "st_wout"))
    for e in range(NE):
        for kt in range(KM):
            r = buf(("scr", "wl1", e, kt)); wl1_regs[e].append(r)
            for hh in range(2):
                cast_jobs.append((d_wl1[e, kt * 128:(kt + 1) * 128, hh * 2048:(hh + 1) * 2048],
                                  s_wl1[e, kt * 128:(kt + 1) * 128, hh * 2048:(hh + 1) * 2048],
                                  2048, r, "st_wl1"))
        for kt in range(0, KI, 2):
            r = buf(("scr", "wl2", e, kt)); wl2_regs[e].append(r)
            cast_jobs.append((d_wl2[e, kt * 128:(kt + 2) * 128, :].rearrange(
                "(a p) m -> p a m", p=128),
                s_wl2[e, kt * 128:(kt + 2) * 128, :].rearrange("(a p) m -> p a m", p=128),
                2048, r, "st_wl2"))
    cast_engs = ("vector", "gpsimd", "scalar")
    cb_b = [buf(("wsl", i)) for i in range(4)]
    fxr_b = [buf(("fxr", i)) for i in range(2)]
    cstg = [(mx_roll, STG, "ld_b2"), (fx_roll, fxr_b, "ld_w3")]
    for ji, (csrc, cdst, cols, reg, stsem) in enumerate(cast_jobs):
        bs = ji % 4
        stg_t, stg_bufs, ldsem = cstg[ji % 2]
        fview = stg_t[:, 0:cols]
        bview = t_w[:, bs * 4096: bs * 4096 + cols]
        fv3 = fview.rearrange("p (a m) -> p a m", a=2) if (cols == 2048 and len(csrc.shape) == 3) else fview
        eng = cast_engs[ji % 3]
        dma("sync", fv3, csrc, stg_bufs, stg_bufs, ldsem, tag=f"cld{ji}")
        if eng == "scalar":
            act(bview, fview, AF.Copy, stg_bufs, [cb_b[bs]], tag=f"cst{ji}")
        else:
            cp(eng, bview, fview, stg_bufs, [cb_b[bs]], tag=f"cst{ji}")
        bv3 = bview.rearrange("p (a m) -> p a m", a=2) if (cols == 2048 and len(cdst.shape) == 3) else bview
        dma("scalar", cdst, bv3, [cb_b[bs]], [reg], stsem, tag=f"cso{ji}")

    fxr_b = [buf(("fxr", i)) for i in range(2)]
    xnt_b = buf("xnt")
    tsu_b = [buf(("su", m)) for m in range(KI)]
    tb2_b = [buf(("b2", m)) for m in range(KM)]
    tg_b = [buf(("tg", m)) for m in range(KI)]
    yat_b = [buf(("yat", m)) for m in range(KM)]
    ymix_b = [buf(("ym", m)) for m in range(KM)]
    bmt_b, cmt_b = buf("bmt"), buf("cmt")
    dg_b = [buf(("dg", d)) for d in range(4)]
    dug_b = [buf(("dug", d)) for d in range(4)]
    da_bb = [buf(("da", i)) for i in range(8)]
    bin_bb = [buf(("bin", i)) for i in range(4)]
    hs_bb = [buf(("hseg", i)) for i in range(4)]
    hp_bb = [buf(("hp", i)) for i in range(8)]
    yg_b = [buf(("yg", d)) for d in range(4)]
    hst_b = [buf(("hst", i)) for i in range(KI * DS)]
    gat_b = buf("gat")
    spa_b = [buf(("spa", d)) for d in range(4)]
    srt_b = [buf(("srt", i)) for i in range(2)]
    pmm_b = [buf("p_mm0"), buf("p_mm1")]
    pax_b = [buf("p_ax0"), buf("p_ax1")]
    pbb_b = buf("p_bb")
    pcf_b = buf("p_cf")
    pt32_b = buf("p_t32")
    pt16_h = [buf("p_t16h0"), buf("p_t16h1")]
    xn_b = buf("xn_t")
    x1n_b = buf("x1n_t")
    x1nt_b = buf("x1nt")
    sab_b = [buf(("sa", i)) for i in range(2)]
    tmp_b = [buf(("tmp", i)) for i in range(2)]
    sbb_b = [buf(("sbb", i)) for i in range(2)]
    wtp_b = [buf(("wtp", i)) for i in range(2)]
    sc_b = buf("sc")
    ew_b = buf("ew")
    ewt_b = buf("ewt")
    ewb_b = buf("ewb")
    tk_b = buf("tk")
    tk1_b = buf("tk1")
    wslot_b = [buf(("wsl", i)) for i in range(4)]
    lnst_b, lnmv_b, lnr_b = buf("lnst"), buf("lnmv"), buf("lnr")

    mxr_ps = [0]

    def mx_ps():
        k = mxr_ps[0] % 2
        mxr_ps[0] += 1
        return (p_t32, pt32_b) if k == 0 else (p_bb, pbb_b)

    ffr_ps = [0]

    def ff_ps():
        k = ffr_ps[0] % 2
        ffr_ps[0] += 1
        return p_ax[k], pax_b[k]

    slot_mx = [0]
    slot_ff = [0]

    def wslot():
        s_ = slot_mx[0] % 2
        slot_mx[0] += 1
        return t_w[:, s_ * 4096:(s_ + 1) * 4096], wslot_b[s_], f"ld_w{s_}"

    def wslot_f():
        s_ = 2 + slot_ff[0] % 2
        slot_ff[0] += 1
        return t_w[:, s_ * 4096:(s_ + 1) * 4096], wslot_b[s_], f"ld_w{s_}"

    def layer_norm(xt_ap, xt_bufs, out_ap, out_buf, tag):
        E("vector", lambda En, o=lnst[:, 0:6], i=xt_ap[:, 0:512]: En.bn_stats(o, i),
          xt_bufs, [lnst_b], tag=tag + "b0")
        E("vector", lambda En, o=lnst[:, 6:12], i=xt_ap[:, 512:1024]: En.bn_stats(o, i),
          xt_bufs, [lnst_b], tag=tag + "b1")
        E("vector", lambda En: En.bn_aggr(lnmv[:, :], lnst[:, 0:12]),
          [lnst_b], [lnmv_b], tag=tag + "ag")
        act(lnr[:, 0:1], lnmv[:, 1:2], AF.Ln, [lnmv_b], [lnr_b], bias=1e-5, tag=tag + "ln")
        act(lnr[:, 1:2], lnr[:, 0:1], AF.Exp, [lnr_b], [lnr_b], scale=-0.5, tag=tag + "ex")
        stt(lnr[:, 2:3], lnmv[:, 0:1], -1.0, lnr[:, 1:2], OP.mult, OP.mult,
            [lnmv_b, lnr_b], [lnr_b], tag=tag + "nm")
        ts("vector", out_ap, xt_ap, lnr[:, 1:2], lnr[:, 2:3], OP.mult, OP.add,
           xt_bufs + [lnr_b], [out_buf], tag=tag + "ap")

    xnt3 = xnt[:, :].rearrange("p (k t) -> p k t", k=KM)
    pt16_3 = p_t16[:, :].rearrange("p (k t) -> p k t", k=KM)
    x1nt3 = x1nt[:, :].rearrange("p (k t) -> p k t", k=KM)

    ystore_bufs = []

    def mixer_chunk(b, c):
        t0 = c * TC
        for j in range(4):
            xr = mx_roll[:, (j % 2) * 1024:(j % 2 + 1) * 1024]
            xrb = mxr_b[j % 2]
            dma("sync", xr, d_x[b, t0 + j * 128: t0 + (j + 1) * 128, :],
                [xrb], [xrb], "ld_x", tag=f"ldx{b}{c}{j}")
            xn_ap = xn_t[:, 0:1024]
            layer_norm(xr, [xrb], xn_ap, xn_b, f"l1_{b}{c}{j}")
            for i in range(KM):
                tp(p_t16[:, i * 128:(i + 1) * 128], xn_ap[:, i * 128:(i + 1) * 128],
                   ident_b[:, :], [xn_b, idb_b], [pt16_h[i // 4]], tag=f"txn{j}_{i}")
            cp("vector", xnt3[:, :, j * 128:(j + 1) * 128], pt16_3,
               [pt16_h[0], pt16_h[1]], [xnt_b], tag=f"exn{j}")
            yield "m"
        for mg in range(4):
            slv, slb, slsem = wslot()
            dma("sync", slv[:, 0:KM * 512].rearrange("p (k c) -> p k c", k=KM),
                s_win[:, mg * 512:(mg + 1) * 512].rearrange("(k p) c -> p k c", p=128),
                win_regs, [slb], slsem, tag=f"lwin{mg}")
            for ml in range(4):
                m = mg * 4 + ml
                pm, pb = mx_ps()
                for k in range(KM):
                    mm(pm[:, :], slv[:, k * 512 + ml * 128: k * 512 + (ml + 1) * 128],
                       xnt[:, k * TC:(k + 1) * TC], k == 0, k == KM - 1,
                       [slb, xnt_b], pb, tag=f"win{m}")
                act(t_su[:, m * TC:(m + 1) * TC], pm[:, :], AF.Silu, [pb], [tsu_b[m]],
                    tag=f"silu{m}")
                yield "m"
        for wsb, wkey, dst, db in ((w_b_sb, "wbs", bmt, bmt_b),
                                   (w_c_sb, "wcs", cmt, cmt_b)):
            pm, pb = mx_ps()
            for k in range(KI):
                mm(pm[:, 0:TC], wsb[:, k * 128:(k + 1) * 128],
                   t_su[:, k * TC:(k + 1) * TC], k == 0, k == KI - 1,
                   [buf(wkey), tsu_b[k]], pb, tag=f"{wkey}{k}")
            act(dst[0:16, :], pm[0:16, 0:TC], AF.Copy, [pb], [db], tag=f"e{wkey}")
            yield "m"
        for g in range(4):
            wd_sl = []
            for half in range(2):
                slv, slb, slsem = wslot()
                dma("sync", slv[:, 0:8 * 512].rearrange("p (k c) -> p k c", k=8),
                    s_wdl[half * 1024:(half + 1) * 1024, g * 512:(g + 1) * 512]
                    .rearrange("(k p) c -> p k c", p=128),
                    wdl_regs, [slb], slsem, tag=f"lwdl{g}{half}")
                wd_sl.append((slv, slb))
            for dt in range(4):
                m = g * 4 + dt
                pm, pb = mx_ps()
                for k in range(KI):
                    slv, slb = wd_sl[k // 8]
                    kk = k % 8
                    mm(pm[:, :], slv[:, kk * 512 + dt * 128: kk * 512 + (dt + 1) * 128],
                       t_su[:, k * TC:(k + 1) * TC], k == 0, k == KI - 1,
                       [slb, tsu_b[k]], pb, tag=f"wdl{m}")
                bcol = bdl_col[:, m:m + 1]
                act(sp_a[:, dt * TC:(dt + 1) * TC], pm[:, :], AF.Exp,
                    [pb, buf("bdl")], [spa_b[dt]], bias=bcol, tag=f"spE{m}")
                yield "m"
            for dt in range(4):
                m = g * 4 + dt
                dga = delta_g[:, dt * TC:(dt + 1) * TC]
                act(dga, sp_a[:, dt * TC:(dt + 1) * TC], AF.Ln, [spa_b[dt]],
                    [dg_b[dt]], bias=1.0, tag=f"spL{m}")
                tt("vector", OP.mult, du_g[:, dt * TC:(dt + 1) * TC], dga,
                   t_su[:, m * TC:(m + 1) * TC], [dg_b[dt], tsu_b[m]], [dug_b[dt]],
                   tag=f"du{m}")
                yield "m"
            for n in range(DS):
                bsl = n % 2
                p_cn, pcn_b = (p_t32, pt32_b) if bsl == 0 else (p_cf, pcf_b)
                mm(p_bb[:, :], onehot[:, n * 128:(n + 1) * 128], bmt[:, :],
                   True, True, [oh_b, bmt_b], pbb_b, tag=f"bcb{g}{n}")
                mm(p_cn[:, :], onehot[:, n * 128:(n + 1) * 128], cmt[:, :],
                   True, True, [oh_b, cmt_b], pcn_b, tag=f"bcc{g}{n}")
                yield "s"
                for dt in range(4):
                    m = g * 4 + dt
                    dsl = dt + 4 * (n % 2)
                    act(da_b[:, dsl * TC:(dsl + 1) * TC],
                        delta_g[:, dt * TC:(dt + 1) * TC], AF.Exp,
                        [dg_b[dt], acol_b], [da_bb[dsl]],
                        scale=a_col[:, m * DS + n: m * DS + n + 1], tag=f"dA{m}_{n}")
                bba4 = bass.AP(p_bb, 0, [[512, 128], [0, 4], [1, TC]])
                tt("vector", OP.mult,
                   bin_b[:, :].rearrange("p (d t) -> p d t", d=4),
                   du_g[:, :].rearrange("p (d t) -> p d t", d=4), bba4,
                   [dug_b[0], dug_b[1], dug_b[2], dug_b[3], pbb_b],
                   [bin_bb[0], bin_bb[1], bin_bb[2], bin_bb[3]], tag=f"bi{g}_{n}")
                yield "s"
                for dt in range(4):
                    idx = (g * 4 + dt) * DS + n
                    dsl = dt + 4 * (n % 2)
                    scan(hseg_b[:, dt * TC:(dt + 1) * TC],
                         da_b[:, dsl * TC:(dsl + 1) * TC],
                         bin_b[:, dt * TC:(dt + 1) * TC], h_state[:, idx:idx + 1],
                         [da_bb[dsl], bin_bb[dt], hst_b[idx]], [hs_bb[dt]],
                         tag=f"sc{g}_{n}")
                yield "s"
                for dt in range(4):
                    idx = (g * 4 + dt) * DS + n
                    cp("vector", h_state[:, idx:idx + 1],
                       hseg_b[:, (dt + 1) * TC - 1:(dt + 1) * TC],
                       [hs_bb[dt]], [hst_b[idx]], tag=f"cr{g}_{n}")
                cba4 = bass.AP(p_cn, 0, [[512, 128], [0, 4], [1, TC]])
                hp_dst = (hp_b[:, bsl * 4 * TC:(bsl + 1) * 4 * TC]
                          if n > 0 else y_g[:, :])
                hp_wb = hp_bb[bsl * 4:bsl * 4 + 4] if n > 0 else yg_b
                tt("vector", OP.mult,
                   hp_dst.rearrange("p (d t) -> p d t", d=4),
                   hseg_b[:, :].rearrange("p (d t) -> p d t", d=4), cba4,
                   [hs_bb[0], hs_bb[1], hs_bb[2], hs_bb[3], pcn_b],
                   [hp_wb[0], hp_wb[1], hp_wb[2], hp_wb[3]], tag=f"hp{g}_{n}")
                yield "s"
                if n > 0:
                    hpp = hp_b[:, bsl * 4 * TC:(bsl + 1) * 4 * TC]
                    tt("gpsimd", OP.add, y_g[:, :], y_g[:, :], hpp,
                       [hp_bb[bsl * 4], hp_bb[bsl * 4 + 1], hp_bb[bsl * 4 + 2],
                        hp_bb[bsl * 4 + 3],
                        yg_b[0], yg_b[1], yg_b[2], yg_b[3]],
                       [yg_b[0], yg_b[1], yg_b[2], yg_b[3]], tag=f"ya{g}_{n}")
                yield "m"
            slv, slb, slsem = wslot()
            dma("sync", slv[:, 0:KM * 512].rearrange("p (k c) -> p k c", k=KM),
                s_win[:, DI + g * 512: DI + (g + 1) * 512].rearrange("(k p) c -> p k c", p=128),
                win_regs, [slb], slsem, tag=f"lwres{g}")
            for dt in range(4):
                m = g * 4 + dt
                pm, pb = mx_ps()
                for k in range(KM):
                    mm(pm[:, :], slv[:, k * 512 + dt * 128: k * 512 + (dt + 1) * 128],
                       xnt[:, k * TC:(k + 1) * TC], k == 0, k == KM - 1,
                       [slb, xnt_b], pb, tag=f"res{m}")
                sra = sres_t[:, (dt % 2) * TC:(dt % 2 + 1) * TC]
                act(sra, pm[:, :], AF.Silu, [pb], [srt_b[dt % 2]], tag=f"sres{m}")
                sua = t_su[:, m * TC:(m + 1) * TC]
                stt(gat_t[:, :], sua, d_colv[:, m:m + 1], y_g[:, dt * TC:(dt + 1) * TC],
                    OP.mult, OP.add, [tsu_b[m], buf("dcol"), yg_b[dt]], [gat_b],
                    tag=f"g1{m}")
                tt("vector", OP.mult, sua, gat_t[:, :], sra,
                   [gat_b, srt_b[dt % 2]], [tsu_b[m]], tag=f"g2{m}")
                yield "m"
        for cg in range(2):
            wo_sl = []
            for half in range(2):
                slv, slb, slsem = wslot()
                dma("sync", slv[:, 0:8 * 512].rearrange("p (k c) -> p k c", k=8),
                    s_wout[half * 1024:(half + 1) * 1024, cg * 512:(cg + 1) * 512]
                    .rearrange("(k p) c -> p k c", p=128),
                    wout_regs, [slb], slsem, tag=f"lwo{cg}{half}")
                wo_sl.append((slv, slb))
            for ml in range(4):
                m = cg * 4 + ml
                pm, pb = mx_ps()
                for k in range(KI):
                    slv, slb = wo_sl[k // 8]
                    kk = k % 8
                    mm(pm[:, :], slv[:, kk * 512 + ml * 128: kk * 512 + (ml + 1) * 128],
                       t_su[:, k * TC:(k + 1) * TC], k == 0, k == KI - 1,
                       [slb, tsu_b[k]], pb, tag=f"wo{m}")
                act(ymix_t[:, m * TC:(m + 1) * TC], pm[:, :], AF.Copy,
                    [pb], [ymix_b[m]], tag=f"ewo{m}")
                yield "m"
        for j in range(4):
            xr = mx_roll[:, (j % 2) * 1024:(j % 2 + 1) * 1024]
            xrb = mxr_b[j % 2]
            dma("sync", xr, d_x[b, t0 + j * 128: t0 + (j + 1) * 128, :],
                [xrb], [xrb], "ld_x", tag=f"rlx{b}{c}{j}")
            for i in range(KM):
                tp(p_t16[:, i * 128:(i + 1) * 128],
                   ymix_t[:, i * TC + j * 128: i * TC + (j + 1) * 128],
                   ident_b[:, :], [ymix_b[i], idb_b], [pt16_h[i // 4]], tag=f"tym{j}{i}")
            tt("vector", OP.add, xr, xr, p_t16[:, :],
               [xrb, pt16_h[0], pt16_h[1]], [xrb], tag=f"res{j}")
            dma("scalar", s_x1[b, t0 + j * 128: t0 + (j + 1) * 128, :], xr,
                [xrb], [buf(("x1s", b, c))], "st_x1", tag=f"sx1{b}{c}{j}")
            xn_ap = x1n_t[:, 0:1024]
            layer_norm(xr, [xrb], xn_ap, x1n_b, f"l2_{b}{c}{j}")
            for i in range(KM):
                tp(p_t16[:, i * 128:(i + 1) * 128], xn_ap[:, i * 128:(i + 1) * 128],
                   ident_b[:, :], [x1n_b, idb_b], [pt16_h[i // 4]], tag=f"tx1n{j}_{i}")
            cp("vector", x1nt3[:, :, j * 128:(j + 1) * 128], pt16_3,
               [pt16_h[0], pt16_h[1]], [x1nt_b], tag=f"ex1n{j}")
            yield "m"
        dma("scalar", s_x1nt[b].rearrange("(k p) t -> p k t", p=128)[:, :, t0:t0 + TC],
            x1nt3, [x1nt_b], [buf(("x1nts", b, c))], "st_x1nt", tag=f"sx1n{b}{c}")

    def ffn_group(b, q):
        t0 = q * TC
        dma("sync", t_b2[:, :].rearrange("p (k t) -> p k t", k=KM),
            s_x1nt[b].rearrange("(k p) t -> p k t", p=128)[:, :, t0:t0 + TC],
            [buf(("x1nts", b, q))] + tb2_b, tb2_b, "ld_b2", tag=f"lfx{b}{q}")
        for tt_ in range(4):
            pm, pb = ff_ps()
            for k in range(KM):
                mm(pm[:, 0:NE], t_b2[:, k * TC + tt_ * 128: k * TC + (tt_ + 1) * 128],
                   wr_sb[:, k * NE:(k + 1) * NE], k == 0, k == KM - 1,
                   [tb2_b[k], buf("wrT")], pb, tag=f"sco{tt_}")
            sca = sc_sb[:, tt_ * NE:(tt_ + 1) * NE]
            tt("vector", OP.add, sca, pm[:, 0:NE], br_bc[:, :],
               [pb, buf("brbc")], [sc_b], tag=f"sce{tt_}")
            red("max", tk1[:, 0:1], sca, [sc_b], [tk1_b], tag="smx")
            ts("vector", tk1[:, 7:8], tk1[:, 0:1], -1.0, None, OP.mult, None,
               [tk1_b], [tk1_b], tag="snm")
            act(tk_sb[:, 0:8], sca, AF.Exp, [sc_b, tk1_b], [tk_b],
                bias=tk1[:, 7:8], tag="sex")
            red("sum", tk1[:, 1:2], tk_sb[:, 0:8], [tk_b], [tk1_b], tag="ssm")
            E("vector", lambda En: En.reciprocal(tk1[:, 2:3], tk1[:, 1:2]),
              [tk1_b], [tk1_b], tag="srp")
            ts("vector", tk_sb[:, 0:8], tk_sb[:, 0:8], tk1[:, 2:3], None, OP.mult, None,
               [tk_b, tk1_b], [tk_b], tag="sw")
            tt("vector", OP.add, tk_sb[:, 8:16], sca, bn_bc[:, :],
               [sc_b, buf("bnbc")], [tk_b], tag="sp1")
            red("max", tk1[:, 3:4], tk_sb[:, 8:16], [tk_b], [tk1_b], tag="sm1")
            ts("vector", tk_sb[:, 32:40], tk_sb[:, 8:16], tk1[:, 3:4], None, OP.is_ge, None,
               [tk_b, tk1_b], [tk_b], tag="sk1")
            stt(tk_sb[:, 16:24], tk_sb[:, 32:40], -1e30, tk_sb[:, 8:16],
                OP.mult, OP.add, [tk_b], [tk_b], tag="sms")
            red("max", tk1[:, 4:5], tk_sb[:, 16:24], [tk_b], [tk1_b], tag="sm2")
            ts("vector", tk_sb[:, 40:48], tk_sb[:, 16:24], tk1[:, 4:5], None, OP.is_ge, None,
               [tk_b, tk1_b], [tk_b], tag="sk2")
            tt("vector", OP.add, tk_sb[:, 32:40], tk_sb[:, 32:40], tk_sb[:, 40:48],
               [tk_b], [tk_b], tag="skm")
            tt("vector", OP.mult, tk_sb[:, 24:32], tk_sb[:, 0:8], tk_sb[:, 32:40],
               [tk_b], [tk_b], tag="svl")
            red("sum", tk1[:, 5:6], tk_sb[:, 24:32], [tk_b], [tk1_b], tag="sdn")
            E("vector", lambda En: En.reciprocal(tk1[:, 6:7], tk1[:, 5:6]),
              [tk1_b], [tk1_b], tag="sdr")
            ts("vector", ew_sb[:, tt_ * NE:(tt_ + 1) * NE], tk_sb[:, 24:32],
               tk1[:, 6:7], None, OP.mult, None, [tk_b, tk1_b], [ew_b], tag=f"sew{tt_}")
            yield
        for e in range(NE):
            for band in range(4):
                slvA, slbA, slsemA = wslot_f()
                dma("sync", slvA[:, 0:KM * 512].rearrange("p (k c) -> p k c", k=KM),
                    s_wl1[e][:, band * 512:(band + 1) * 512]
                    .rearrange("(k p) c -> p k c", p=128),
                    wl1_regs[e], [slbA], slsemA, tag=f"lA{e}{band}")
                slvB, slbB, slsemB = wslot_f()
                dma("sync", slvB[:, 0:KM * 512].rearrange("p (k c) -> p k c", k=KM),
                    s_wl1[e][:, H + band * 512: H + (band + 1) * 512]
                    .rearrange("(k p) c -> p k c", p=128),
                    wl1_regs[e], [slbB], slsemB, tag=f"lB{e}{band}")
                for ml in range(4):
                    m = band * 4 + ml
                    pa, pab = p_mm[m % 2], pmm_b[m % 2]
                    pbv, pbb2 = p_ax[m % 2], pax_b[m % 2]
                    for k in range(KM):
                        mm(pa[:, :], slvA[:, k * 512 + ml * 128: k * 512 + (ml + 1) * 128],
                           t_b2[:, k * TC:(k + 1) * TC], k == 0, k == KM - 1,
                           [slbA, tb2_b[k]], pab, tag=f"h1a{e}{m}")
                    for k in range(KM):
                        mm(pbv[:, :], slvB[:, k * 512 + ml * 128: k * 512 + (ml + 1) * 128],
                           t_b2[:, k * TC:(k + 1) * TC], k == 0, k == KM - 1,
                           [slbB, tb2_b[k]], pbb2, tag=f"h1b{e}{m}")
                    sl = m % 2
                    saa = sa_b[:, sl * TC:(sl + 1) * TC]
                    sba = sb_b2[:, sl * TC:(sl + 1) * TC]
                    act(saa, pa[:, :], AF.Silu, [pab, buf("bl1")],
                        [sab_b[sl]], bias=bl1_col[:, e * MH + m: e * MH + m + 1],
                        tag=f"sl{e}{m}")
                    act(sba, pbv[:, :], AF.Identity, [pbb2, buf("bl1")],
                        [sbb_b[sl]], bias=bl1_col[:, e * MH + 16 + m: e * MH + 17 + m],
                        tag=f"sb{e}{m}")
                    tt("gpsimd", OP.mult, t_g[:, m * TC:(m + 1) * TC], saa, sba,
                       [sab_b[sl], sbb_b[sl]], [tg_b[m]], tag=f"gl{e}{m}")
                    yield
            if e == 0:
                for tt_ in range(4):
                    pm2, pb2 = ff_ps()
                    tp(pm2[0:8, 0:128], ew_sb[:, tt_ * NE:(tt_ + 1) * NE],
                       ident_f[:, :], [ew_b, idf_b], [pb2], tag=f"ewt{tt_}")
                    cp("vector", ewt_sb[0:8, tt_ * 128:(tt_ + 1) * 128],
                       pm2[0:8, 0:128], [pb2], [ewt_b], tag=f"ewte{tt_}")
                yield
            mm(p_mm[0][:, :], onehot[:, e * 128:(e + 1) * 128], ewt_sb[:, :],
               True, True, [oh_b, ewt_b], pmm_b[0], tag=f"ewb{e}")
            act(ewb_sb[:, :], p_mm[0][:, :], AF.Copy, [pmm_b[0]], [ewb_b], tag=f"ewbe{e}")
            yield
            for mp in range(4):
                slv, slb, slsem = wslot_f()
                dma("sync", slv[:, 0:KI * 256].rearrange("p (k c) -> p k c", k=KI),
                    s_wl2[e][:, mp * 256:(mp + 1) * 256]
                    .rearrange("(k p) c -> p k c", p=128),
                    wl2_regs[e], [slb], slsem, tag=f"lW2{e}{mp}")
                for m2l in range(2):
                    m2 = mp * 2 + m2l
                    pm, pb = p_mm[m2 % 2], pmm_b[m2 % 2]
                    for k in range(KI):
                        mm(pm[:, :], slv[:, k * 256 + m2l * 128: k * 256 + (m2l + 1) * 128],
                           t_g[:, k * TC:(k + 1) * TC], k == 0, k == KI - 1,
                           [slb, tg_b[k]], pb, tag=f"w2{e}{m2}")
                    ya = y_acct[:, m2 * TC:(m2 + 1) * TC]
                    sl = m2 % 2
                    if e == 0:
                        stt(ya, pm[:, :], bl2_col[:, e * KM + m2: e * KM + m2 + 1],
                            ewb_sb[:, :], OP.add, OP.mult,
                            [pb, buf("bl2"), ewb_b], [yat_b[m2]], tag=f"yi{e}{m2}")
                    else:
                        tmpa = tmp_s[:, sl * TC:(sl + 1) * TC]
                        stt(tmpa, pm[:, :], bl2_col[:, e * KM + m2: e * KM + m2 + 1],
                            ewb_sb[:, :], OP.add, OP.mult,
                            [pb, buf("bl2"), ewb_b], [tmp_b[sl]], tag=f"ym{e}{m2}")
                        if m2l == 1:
                            yp = y_acct[:, (m2 - 1) * TC:(m2 + 1) * TC]
                            tt("gpsimd", OP.add, yp, yp, tmp_s[:, :],
                               [tmp_b[0], tmp_b[1], yat_b[m2 - 1], yat_b[m2]],
                               [yat_b[m2 - 1], yat_b[m2]], tag=f"yA{e}{m2}")
                    yield
        for j in range(4):
            xr = fx_roll[:, (j % 2) * 1024:(j % 2 + 1) * 1024]
            xrb = fxr_b[j % 2]
            dma("sync", xr, s_x1[b, t0 + j * 128: t0 + (j + 1) * 128, :],
                [buf(("x1s", b, q)), xrb], [xrb], "ld_b2", tag=f"lX1{b}{q}{j}")
            for half in range(2):
                pm, pb = ff_ps()
                for i in range(4):
                    mdm = half * 4 + i
                    tp(pm[:, i * 128:(i + 1) * 128],
                       y_acct[:, mdm * TC + j * 128: mdm * TC + (j + 1) * 128],
                       ident_f[:, :], [yat_b[mdm], idf_b], [pb], tag=f"tya{j}{half}")
                xh = xr[:, half * 512:(half + 1) * 512]
                tt("vector", OP.add, xh, xh, pm[:, :],
                   [xrb, pb], [xrb], tag=f"fo{j}{half}")
            yb = buf(("ys", b, q, j))
            dma("scalar", d_y[b, t0 + j * 128: t0 + (j + 1) * 128, :], xr,
                [xrb], [yb], "st_y", tag=f"sy{b}{q}{j}")
            ystore_bufs.append(yb)
            yield

    pend = []
    ffg = [None]

    def ffn_step(k):
        for _ in range(k):
            if ffg[0] is None:
                if not pend:
                    return
                ffg[0] = ffn_group(*pend.pop(0))
            try:
                next(ffg[0])
            except StopIteration:
                ffg[0] = None

    for b in range(BPC):
        mset(h_state[:, :], 0.0, hst_b, tag=f"hz{b}")
        for c in range(NCH):
            mg = mixer_chunk(b, c)
            alive = True
            while alive:
                for _ in range(5):
                    try:
                        next(mg)
                    except StopIteration:
                        alive = False
                        break
                ffn_step(2)
            pend.append((b, c))
    while pend or ffg[0] is not None:
        ffn_step(8)

    E("sync", lambda En: En.nop(), ystore_bufs, [], tag="final_wait")

    streams, tok, dcnt, ecnt = S.build_streams()
    dma_sem_map = {k: sems[k] for k in sems if k.startswith(("ld_", "st_"))}
    eng_sem_map = {e: sems[k] for e, k in
                   (("tensor", "pe"), ("vector", "v"), ("scalar", "s"),
                    ("gpsimd", "g"), ("sync", "pe"))}

    with es:
        with nc.Block() as blk0:
            @blk0.sync
            def _(sy):
                for s_ in sems.values():
                    sy.sem_clear(s_)

        with nc.Block() as blk:
            def run_stream(eng_name):
                def body(En):
                    ops = S.ops
                    for cmd in streams[eng_name]:
                        if cmd[0] == "wait":
                            (kind, key), val = cmd[1], cmd[2]
                            sem = dma_sem_map[key] if kind == "d" else eng_sem_map[key]
                            En.wait_ge(sem, val)
                        elif cmd[0] == "drain":
                            En.drain()
                        else:
                            i = cmd[1]
                            inst = ops[i][2](En)
                            tk = tok[i]
                            if tk is not None:
                                if tk[0] == "d":
                                    inst.then_inc(dma_sem_map[tk[1]], 16)
                                else:
                                    inst.then_inc(eng_sem_map[tk[1]], 1)
                return body

            blk.sync(run_stream("sync"))
            blk.tensor(run_stream("tensor"))
            blk.vector(run_stream("vector"))
            blk.scalar(run_stream("scalar"))
            blk.gpsimd(run_stream("gpsimd"))

    return nc


_CACHE = {}


def kernel(**inputs):
    if "nc" not in _CACHE:
        _CACHE["nc"] = build_nc()
    nc = _CACHE["nc"]
    key_map = {
        "x": "L0_attn_x", "w_in": "L0_attn_W_in", "w_delta": "L0_attn_W_delta",
        "b_delta": "L0_attn_b_delta", "w_b": "L0_attn_W_B", "w_c": "L0_attn_W_C",
        "a_log": "L0_attn_A_log", "d_param": "L0_attn_D_param",
        "w_out": "L0_attn_W_out", "wr": "L0_ffn_Wr", "br": "L0_ffn_br",
        "wl1": "L0_ffn_Wl1", "bl1": "L0_ffn_bl1", "wl2": "L0_ffn_Wl2",
        "bl2": "L0_ffn_bl2", "biases_n": "L0_ffn_biases_N",
    }
    full = {k: np.ascontiguousarray(np.asarray(inputs[v], dtype=np.float32))
            for k, v in key_map.items()}
    in_maps = []
    for c in range(NCORES):
        m = dict(full)
        m["x"] = np.ascontiguousarray(full["x"][c * BPC:(c + 1) * BPC])
        in_maps.append(m)
    res = run_bass_kernel_spmd(nc, in_maps, core_ids=list(range(NCORES)))
    out = np.concatenate([np.asarray(res.results[i]["y"]) for i in range(NCORES)], axis=0)
    return out.astype(np.float32, copy=False)


# version 18
# speedup vs baseline: 1.0323x; 1.0323x over previous
import contextlib
import numpy as np
import ml_dtypes
import concourse.bass as bass
import concourse.mybir as mybir
from concourse.bass_utils import run_bass_kernel_spmd
from concourse.alu_op_type import AluOpType

F32 = mybir.dt.float32
BF16 = mybir.dt.bfloat16
F16 = mybir.dt.float16
AF = mybir.ActivationFunctionType
AX = mybir.AxisListType
OP = AluOpType

NCORES = 8
BPC = 4
T = 2048
TC = 512
NCH = T // TC
DM = 1024
KM = DM // 128
DI = 2048
KI = DI // 128
DS = 16
NE = 8
H = 2048
MH = 32

ENGS = ("sync", "tensor", "vector", "scalar", "gpsimd")


class Buf:
    __slots__ = ("writer", "readers")

    def __init__(self):
        self.writer = None
        self.readers = []


class Sched:
    def __init__(self):
        self.ops = []
        self.last_dma = {}

    def op(self, eng, fn, reads=(), writes=(), dma=None, tag=""):
        i = len(self.ops)
        deps = set()
        if dma is not None and dma in self.last_dma:
            deps.add(self.last_dma[dma])
        for b in reads:
            if b.writer is not None:
                deps.add(b.writer)
        for b in writes:
            if b.writer is not None:
                deps.add(b.writer)
            deps.update(b.readers)
        deps.discard(i)
        self.ops.append([i, eng, fn, deps, dma, tag])
        if dma is not None:
            self.last_dma[dma] = i
        for b in reads:
            b.readers.append(i)
        for b in writes:
            b.writer = i
            b.readers = []
        return i

    def build_streams(self):
        ops = self.ops
        n = len(ops)
        has_cross = [False] * n
        for i, eng, fn, deps, dma, tag in ops:
            for d in deps:
                if ops[d][1] != eng or eng != "tensor":
                    has_cross[d] = True
        ecnt = {e: 0 for e in ENGS}
        dcnt = {}
        tok = [None] * n
        for i, eng, fn, deps, dma, tag in ops:
            if dma is not None:
                dcnt[dma] = dcnt.get(dma, 0) + 16
                tok[i] = ("d", dma, dcnt[dma])
            elif has_cross[i]:
                ecnt[eng] += 1
                tok[i] = ("e", eng, ecnt[eng])
        streams = {e: [] for e in ENGS}
        water = {e: {} for e in ENGS}
        drained_upto = {e: -1 for e in ENGS}
        last_own = {e: -1 for e in ENGS}
        run_dcnt = {}
        for i, eng, fn, deps, dma, tag in ops:
            st = streams[eng]
            need_drain = False
            for d in sorted(deps):
                deng, ddma = ops[d][1], ops[d][4]
                if ddma is not None:
                    val = run_dcnt.get(ddma, 0)
                    assert val >= tok[d][2], f"dma dep {d} after waiter {i}?"
                    key = ("d", ddma)
                    if water[eng].get(key, -1) < val:
                        water[eng][key] = val
                        st.append(("wait", key, val))
                elif deng != eng or eng != "tensor":
                    tk = tok[d]
                    assert tk is not None, f"dep {d} ({ops[d][5]}) of {i} ({tag}) untokenized"
                    key = (tk[0], tk[1])
                    if water[eng].get(key, -1) < tk[2]:
                        water[eng][key] = tk[2]
                        st.append(("wait", key, tk[2]))
                else:
                    pass
            if need_drain:
                st.append(("drain",))
                drained_upto[eng] = last_own[eng]
            st.append(("op", i))
            if dma is not None:
                run_dcnt[dma] = run_dcnt.get(dma, 0) + 16
            else:
                last_own[eng] = i
        return streams, tok, dcnt, ecnt


def build_nc():
    nc = bass.Bass("TRN2", target_bir_lowering=False)

    d_x = nc.dram_tensor("x", [BPC, T, DM], F32, kind="ExternalInput").ap()
    d_win = nc.dram_tensor("w_in", [DM, 2 * DI], F32, kind="ExternalInput").ap()
    d_wdl = nc.dram_tensor("w_delta", [DI, DI], F32, kind="ExternalInput").ap()
    d_bdl = nc.dram_tensor("b_delta", [DI], F32, kind="ExternalInput").ap()
    d_wb = nc.dram_tensor("w_b", [DI, DS], F32, kind="ExternalInput").ap()
    d_wc = nc.dram_tensor("w_c", [DI, DS], F32, kind="ExternalInput").ap()
    d_alog = nc.dram_tensor("a_log", [DI, DS], F32, kind="ExternalInput").ap()
    d_dpar = nc.dram_tensor("d_param", [DI], F32, kind="ExternalInput").ap()
    d_wout = nc.dram_tensor("w_out", [DI, DM], F32, kind="ExternalInput").ap()
    d_wr = nc.dram_tensor("wr", [NE, DM], F32, kind="ExternalInput").ap()
    d_br = nc.dram_tensor("br", [NE], F32, kind="ExternalInput").ap()
    d_wl1 = nc.dram_tensor("wl1", [NE, DM, 2 * H], F32, kind="ExternalInput").ap()
    d_bl1 = nc.dram_tensor("bl1", [NE, 2 * H], F32, kind="ExternalInput").ap()
    d_wl2 = nc.dram_tensor("wl2", [NE, H, DM], F32, kind="ExternalInput").ap()
    d_bl2 = nc.dram_tensor("bl2", [NE, DM], F32, kind="ExternalInput").ap()
    d_bn = nc.dram_tensor("biases_n", [NE], F32, kind="ExternalInput").ap()
    d_y = nc.dram_tensor("y", [BPC, T, DM], F32, kind="ExternalOutput").ap()

    s_win = nc.dram_tensor("s_win", [DM, 2 * DI], BF16).ap()
    s_wdl = nc.dram_tensor("s_wdl", [DI, DI], BF16).ap()
    s_wout = nc.dram_tensor("s_wout", [DI, DM], BF16).ap()
    s_wl1 = nc.dram_tensor("s_wl1", [NE, DM, 2 * H], BF16).ap()
    s_wl2 = nc.dram_tensor("s_wl2", [NE, H, DM], BF16).ap()
    s_x1 = nc.dram_tensor("s_x1", [BPC, T, DM], F32).ap()
    s_x1nt = nc.dram_tensor("s_x1nt", [BPC, DM, T], BF16).ap()

    es = contextlib.ExitStack()

    def sbuf(name, p, f, dt):
        return es.enter_context(nc.sbuf_tensor(name, [p, f], dt))

    def psum(name, p, f, dt):
        return es.enter_context(nc.psum_tensor(name, [p, f], dt))

    cpool = sbuf("cpool", 128, 8, F32)
    ident_b = sbuf("ident_b", 128, 128, BF16)
    ident_f = sbuf("ident_f", 128, 128, F32)
    onehot = sbuf("onehot", 128, 16 * 128, F16)
    a_col = sbuf("a_col", 128, KI * DS, F32)
    d_colv = sbuf("d_colv", 128, KI, F32)
    bdl_col = sbuf("bdl_col", 128, KI, F32)
    bl1_col = sbuf("bl1_col", 128, NE * MH, F32)
    bl2_col = sbuf("bl2_col", 128, NE * KM, F32)
    br_row = sbuf("br_row", 1, NE, F32)
    bn_row = sbuf("bn_row", 1, NE, F32)
    ones_r = sbuf("ones_r", 1, 128, F32)
    br_bc = sbuf("br_bc", 128, NE, F32)
    bn_bc = sbuf("bn_bc", 128, NE, F32)
    wr_sb = sbuf("wr_sb", 128, KM * NE, BF16)
    wr_tmp = sbuf("wr_tmp", 8, DM, BF16)
    w_b_sb = sbuf("w_b_sb", 128, KI * 128, BF16)
    w_c_sb = sbuf("w_c_sb", 128, KI * 128, BF16)
    h_state = sbuf("h_state", 128, KI * DS, F32)

    mx_roll = sbuf("mx_roll", 128, 2048, F32)
    xn_t = sbuf("xn_t", 128, 1024, BF16)
    xnt = sbuf("xnt", 128, KM * TC, BF16)
    t_su = sbuf("t_su", 128, KI * TC, BF16)
    delta_g = sbuf("delta_g", 128, 4 * TC, F16)
    du_g = sbuf("du_g", 128, 4 * TC, F16)
    da_b = sbuf("da_b", 128, 8 * TC, F16)
    bin_b = sbuf("bin_b", 128, 4 * TC, F16)
    hseg_b = sbuf("hseg_b", 128, 4 * TC, F16)
    hp_b = sbuf("hp_b", 128, 4 * TC, F16)
    bbs = sbuf("bbs", 128, 2 * TC, F16)
    cbs = sbuf("cbs", 128, 2 * TC, F16)
    y_g = sbuf("y_g", 128, 4 * TC, F16)
    gat_t = sbuf("gat_t", 128, TC, F16)
    sp_a = sbuf("sp_a", 128, TC, F16)
    sres_t = sbuf("sres_t", 128, 2 * TC, BF16)
    bmt = sbuf("bmt", 128, TC, F16)
    cmt = sbuf("cmt", 128, TC, F16)
    x1n_t = sbuf("x1n_t", 128, 1024, BF16)
    x1nt = sbuf("x1nt", 128, KM * TC, BF16)
    ymix_t = sbuf("ymix_t", 128, KM * TC, BF16)
    lnst = sbuf("lnst", 128, 12, F32)
    lnmv = sbuf("lnmv", 128, 2, F32)
    lnr = sbuf("lnr", 128, 4, F32)
    fx_roll = sbuf("fx_roll", 128, 2048, F32)
    t_b2 = sbuf("t_b2", 128, KM * TC, BF16)
    t_g = sbuf("t_g", 128, KI * TC, BF16)
    y_acct = sbuf("y_acct", 128, KM * TC, F32)
    sa_b = sbuf("sa_b", 128, 2 * TC, BF16)
    sb_b2 = sbuf("sb_b2", 128, 2 * TC, BF16)
    wtmp = sbuf("wtmp", 128, 2 * TC, BF16)
    tmp_s = sbuf("tmp_s", 128, 2 * TC, F16)
    ewt_sb = sbuf("ewt_sb", 128, TC, F16)
    ewb_sb = sbuf("ewb_sb", 128, TC, F16)
    sc_sb = sbuf("sc_sb", 128, 4 * NE, F32)
    ew_sb = sbuf("ew_sb", 128, 4 * NE, F32)
    tk_sb = sbuf("tk_sb", 128, 6 * NE, F32)
    tk1 = sbuf("tk1", 128, 8, F32)
    t_w = sbuf("t_w", 128, 4 * 4096, BF16)

    p_mm = [psum("p_mm0", 128, 512, F32), psum("p_mm1", 128, 512, F32)]
    p_ax = [psum("p_ax0", 128, 512, F32), psum("p_ax1", 128, 512, F32)]
    p_bb = psum("p_bb", 128, 512, F32)
    p_cb = psum("p_cb", 128, 512, F32)
    p_t32 = psum("p_t32", 128, 512, F32)
    p_t16 = psum("p_t16", 128, 1024, BF16)

    sems = {}
    for name in ("pe", "v", "s", "g",
                 "ld_x", "ld_b2", "ld_w0", "ld_w1", "ld_w2", "ld_w3",
                 "st_win", "st_wdl", "st_wout", "st_wl1", "st_wl2",
                 "st_x1", "st_x1nt", "st_y"):
        sems[name] = es.enter_context(nc.semaphore("sem_" + name))

    S = Sched()
    B = {}

    def buf(key):
        if key not in B:
            B[key] = Buf()
        return B[key]

    def E(eng, fn, reads=(), writes=(), dma=None, tag=""):
        return S.op(eng, fn, reads, writes, dma, tag)

    def mm(out_ap, lhs_ap, rhs_ap, start, stop, reads, wbuf, tag=""):
        E("tensor", lambda En, o=out_ap, l=lhs_ap, r=rhs_ap, a=start, z=stop:
          En.matmul(o, l, r, start=a, stop=z), reads, [wbuf], tag=tag)

    def tp(out_ap, in_ap, id_ap, reads, writes, tag=""):
        E("tensor", lambda En, o=out_ap, i=in_ap, d=id_ap: En.transpose(o, i, d),
          reads, writes, tag=tag)

    def tt(eng, alu, out_ap, a_ap, b_ap, reads, writes, tag=""):
        E(eng, lambda En, o=out_ap, a=a_ap, b=b_ap, u=alu: En.tensor_tensor(o, a, b, u),
          reads, writes, tag=tag)

    def ts(eng, out_ap, in_ap, s1, s2, op0, op1, reads, writes, tag=""):
        if op1 is None:
            E(eng, lambda En, o=out_ap, i=in_ap, a=s1, u=op0: En.tensor_scalar(o, i, a, None, u),
              reads, writes, tag=tag)
        else:
            E(eng, lambda En, o=out_ap, i=in_ap, a=s1, b=s2, u=op0, w=op1:
              En.tensor_scalar(o, i, a, b, u, w), reads, writes, tag=tag)

    def stt(out_ap, in0_ap, sc, in1_ap, op0, op1, reads, writes, tag=""):
        E("vector", lambda En, o=out_ap, i=in0_ap, s=sc, j=in1_ap, u=op0, w=op1:
          En.scalar_tensor_tensor(o, i, s, j, u, w), reads, writes, tag=tag)

    def act(out_ap, in_ap, func, reads, writes, bias=None, scale=None, tag=""):
        kw = {}
        if bias is not None:
            kw["bias"] = bias
        if scale is not None:
            kw["scale"] = scale
        E("scalar", lambda En, o=out_ap, i=in_ap, f=func, k=kw: En.activation(o, i, f, **k),
          reads, writes, tag=tag)

    def cp(eng, out_ap, in_ap, reads, writes, tag=""):
        E(eng, lambda En, o=out_ap, i=in_ap: En.tensor_copy(o, i), reads, writes, tag=tag)

    def red(kind, out_ap, in_ap, reads, writes, tag=""):
        if kind == "max":
            E("vector", lambda En, o=out_ap, i=in_ap: En.reduce_max(o, i, axis=AX.X),
              reads, writes, tag=tag)
        else:
            E("vector", lambda En, o=out_ap, i=in_ap: En.reduce_sum(o, i, axis=AX.X),
              reads, writes, tag=tag)

    def scan(out_ap, d0, d1, init_ap, reads, writes, tag=""):
        E("vector", lambda En, o=out_ap, a=d0, b=d1, i=init_ap:
          En.tensor_tensor_scan(o, a, b, i, OP.mult, OP.add), reads, writes, tag=tag)

    def dma(q, out_ap, in_ap, reads, writes, sem, tag=""):
        E(q, lambda En, o=out_ap, i=in_ap: En.dma_start(o, i), reads, writes, dma=sem, tag=tag)

    def mset(ap, val, writes, tag=""):
        E("gpsimd", lambda En, a=ap, v=val: En.memset(a, v), (), writes, tag=tag)

    nc.const_aps.register(F32, 1.0, cpool[:, 0:1])
    nc.const_aps.register(F32, -0.5, cpool[:, 1:2])
    nc.const_aps.register(F32, 1e-5, cpool[:, 2:3])
    nc.const_aps.register(F32, -1.0, cpool[:, 3:4])

    mxr_b = [buf(("mxr", i)) for i in range(2)]
    STG = mxr_b

    cp_b = buf("cpool")
    for j, val in enumerate([1.0, -0.5, 1e-5, -1.0]):
        mset(cpool[:, j:j + 1], val, [cp_b], tag="cpool")
    mset(ones_r[:, :], 1.0, [buf("ones")], tag="ones")
    idf_b, idb_b = buf("identf"), buf("identb")
    E("gpsimd", lambda En: En.iota(ident_f[:, :], [[1, 128]], channel_multiplier=-1,
                                   allow_small_or_imprecise_dtypes=True),
      (), [idf_b], tag="iota")
    ts("gpsimd", ident_f[:, :], ident_f[:, :], 0.0, None, OP.is_equal, None,
       [idf_b], [idf_b], tag="identf")
    cp("gpsimd", ident_b[:, :], ident_f[:, :], [idf_b], [idb_b], tag="identb")
    oh_b = buf("oh")
    E("gpsimd", lambda En: En.iota(mx_roll[0:128, 0:2048], [[1, 16], [0, 128]],
                                   channel_multiplier=-1,
                                   allow_small_or_imprecise_dtypes=True),
      (), STG, tag="ohiota")
    ts("gpsimd", onehot[:, :], mx_roll[0:128, 0:2048], 0.0, None, OP.is_equal, None,
       STG, [oh_b], tag="onehot")

    dma("sync", mx_roll[:, 0:KI * DS].rearrange("p (m n) -> p m n", m=KI),
        d_alog.rearrange("(m p) n -> p m n", p=128), STG, STG, "ld_x", tag="ld_alog")
    acol_b = buf("acol")
    act(a_col[:, :], mx_roll[:, 0:KI * DS], AF.Exp, STG, [acol_b], tag="a_exp")
    ts("vector", a_col[:, :], a_col[:, :], -1.0, None, OP.mult, None,
       [acol_b], [acol_b], tag="a_neg")
    pt32_b0 = buf("p_t32")
    idf_b0 = buf("identf")

    def vec_load(dst_ap, dst_key, src_ap, rows, tg):
        dma("sync", mx_roll[0:rows, 0:128], src_ap, STG, STG, "ld_x", tag=tg + "l")
        tp(p_t32[:, 0:rows], mx_roll[0:rows, 0:128], ident_f[0:rows, 0:rows],
           STG + [idf_b0], [pt32_b0], tag=tg + "t")
        cp("vector", dst_ap, p_t32[:, 0:rows], [pt32_b0], [buf(dst_key)], tag=tg + "e")

    vec_load(d_colv[:, :], "dcol", d_dpar.rearrange("(m p) -> m p", p=128), 16, "vdp")
    vec_load(bdl_col[:, :], "bdl", d_bdl.rearrange("(m p) -> m p", p=128), 16, "vbd")
    for e_ in range(NE):
        vec_load(bl1_col[:, e_ * MH:(e_ + 1) * MH], "bl1",
                 d_bl1[e_, :].rearrange("(m p) -> m p", p=128), 32, f"vb1{e_}")
        vec_load(bl2_col[:, e_ * KM:(e_ + 1) * KM], "bl2",
                 d_bl2[e_, :].rearrange("(m p) -> m p", p=128), 8, f"vb2{e_}")
    dma("sync", br_row[:, :], d_br.rearrange("(a n) -> a n", a=1), (), [buf("brrow")], "ld_x")
    dma("sync", bn_row[:, :], d_bn.rearrange("(a n) -> a n", a=1), (), [buf("bnrow")], "ld_x")
    mset(w_b_sb[:, :], 0.0, [buf("wbs")], tag="wbz")
    mset(w_c_sb[:, :], 0.0, [buf("wcs")], tag="wcz")
    mset(bmt[:, :], 0.0, [buf("bmt")], tag="bmz")
    mset(cmt[:, :], 0.0, [buf("cmt")], tag="cmz")
    mset(ewt_sb[:, :], 0.0, [buf("ewt")], tag="ewz")
    dma("sync", mx_roll[:, 0:KI * DS].rearrange("p (m n) -> p m n", m=KI),
        d_wb.rearrange("(m p) n -> p m n", p=128), STG, STG, "ld_x")
    cp("vector", w_b_sb[:, :].rearrange("p (k c) -> p k c", k=KI)[:, :, 0:DS],
       mx_roll[:, 0:KI * DS].rearrange("p (k c) -> p k c", k=KI), STG, [buf("wbs")], tag="wb_cast")
    dma("sync", mx_roll[:, 0:KI * DS].rearrange("p (m n) -> p m n", m=KI),
        d_wc.rearrange("(m p) n -> p m n", p=128), STG, STG, "ld_x")
    cp("vector", w_c_sb[:, :].rearrange("p (k c) -> p k c", k=KI)[:, :, 0:DS],
       mx_roll[:, 0:KI * DS].rearrange("p (k c) -> p k c", k=KI), STG, [buf("wcs")], tag="wc_cast")
    dma("sync", mx_roll[0:8, 0:DM], d_wr, STG, STG, "ld_x")
    cp("vector", wr_tmp[:, :], mx_roll[0:8, 0:DM], STG, [buf("wrb")], tag="wr_cast")

    for row_b, row, dst, dkey in ((buf("brrow"), br_row, br_bc, "brbc"),
                                  (buf("bnrow"), bn_row, bn_bc, "bnbc")):
        mm(p_ax[0][:, 0:NE], ones_r[0:1, :], row[0:1, :], True, True,
           [row_b, buf("ones")], buf("p_ax0"), tag="bc_" + dkey)
        cp("vector", dst[:, :], p_ax[0][:, 0:NE], [buf("p_ax0")], [buf(dkey)], tag="bce_" + dkey)

    for k in range(KM):
        tp(p_t16[:, 0:NE], wr_tmp[:, k * 128:(k + 1) * 128], ident_b[0:8, 0:8],
           [buf("wrb"), idb_b], [buf("p_t16")], tag=f"wr_tp{k}")
        cp("vector", wr_sb[:, k * NE:(k + 1) * NE], p_t16[:, 0:NE],
           [buf("p_t16")], [buf("wrT")], tag=f"wr_ev{k}")

    cast_jobs = []
    win_regs, wdl_regs, wout_regs = [], [], []
    wl1_regs = [[] for _ in range(NE)]
    wl2_regs = [[] for _ in range(NE)]
    for kt in range(KM):
        r = buf(("scr", "win", kt)); win_regs.append(r)
        for hh in range(2):
            cast_jobs.append((d_win[kt * 128:(kt + 1) * 128, hh * 2048:(hh + 1) * 2048],
                              s_win[kt * 128:(kt + 1) * 128, hh * 2048:(hh + 1) * 2048],
                              2048, r, "st_win"))
    for kt in range(KI):
        r = buf(("scr", "wdl", kt)); wdl_regs.append(r)
        cast_jobs.append((d_wdl[kt * 128:(kt + 1) * 128, :],
                          s_wdl[kt * 128:(kt + 1) * 128, :], 2048, r, "st_wdl"))
    for kt in range(KI):
        r = buf(("scr", "wout", kt)); wout_regs.append(r)
        cast_jobs.append((d_wout[kt * 128:(kt + 1) * 128, :],
                          s_wout[kt * 128:(kt + 1) * 128, :], 1024, r, "st_wout"))
    for e in range(NE):
        for kt in range(KM):
            r = buf(("scr", "wl1", e, kt)); wl1_regs[e].append(r)
            for hh in range(2):
                cast_jobs.append((d_wl1[e, kt * 128:(kt + 1) * 128, hh * 2048:(hh + 1) * 2048],
                                  s_wl1[e, kt * 128:(kt + 1) * 128, hh * 2048:(hh + 1) * 2048],
                                  2048, r, "st_wl1"))
        for kt in range(0, KI, 2):
            r = buf(("scr", "wl2", e, kt)); wl2_regs[e].append(r)
            cast_jobs.append((d_wl2[e, kt * 128:(kt + 2) * 128, :].rearrange(
                "(a p) m -> p a m", p=128),
                s_wl2[e, kt * 128:(kt + 2) * 128, :].rearrange("(a p) m -> p a m", p=128),
                2048, r, "st_wl2"))
    cast_engs = ("vector", "gpsimd", "scalar")
    cb_b = [buf(("wsl", i)) for i in range(4)]
    cfxr_b = [buf(("fxr", i)) for i in range(2)]
    cstg = [(mx_roll, STG, "ld_b2"), (fx_roll, cfxr_b, "ld_w3")]
    for ji, (csrc, cdst, cols, reg, stsem) in enumerate(cast_jobs):
        bs = ji % 4
        stg_t, stg_bufs, ldsem = cstg[ji % 2]
        fview = stg_t[:, 0:cols]
        bview = t_w[:, bs * 4096: bs * 4096 + cols]
        fv3 = fview.rearrange("p (a m) -> p a m", a=2) if (cols == 2048 and len(csrc.shape) == 3) else fview
        eng = cast_engs[ji % 3]
        dma("sync", fv3, csrc, stg_bufs, stg_bufs, ldsem, tag=f"cld{ji}")
        if eng == "scalar":
            act(bview, fview, AF.Copy, stg_bufs, [cb_b[bs]], tag=f"cst{ji}")
        else:
            cp(eng, bview, fview, stg_bufs, [cb_b[bs]], tag=f"cst{ji}")
        bv3 = bview.rearrange("p (a m) -> p a m", a=2) if (cols == 2048 and len(cdst.shape) == 3) else bview
        dma("scalar", cdst, bv3, [cb_b[bs]], [reg], stsem, tag=f"cso{ji}")

    fxr_b = [buf(("fxr", i)) for i in range(2)]
    xnt_b = buf("xnt")
    tsu_b = [buf(("su", m)) for m in range(KI)]
    tb2_b = [buf(("b2", m)) for m in range(KM)]
    tg_b = [buf(("tg", m)) for m in range(KI)]
    yat_b = [buf(("yat", m)) for m in range(KM)]
    ymix_b = [buf(("ym", m)) for m in range(KM)]
    bmt_b, cmt_b = buf("bmt"), buf("cmt")
    dg_b = [buf(("dg", d)) for d in range(4)]
    dug_b = [buf(("dug", d)) for d in range(4)]
    da_bb = [buf(("da", i)) for i in range(8)]
    bin_bb = [buf(("bin", i)) for i in range(4)]
    hs_bb = [buf(("hseg", i)) for i in range(4)]
    hp_bb = [buf(("hp", i)) for i in range(4)]
    yg_b = [buf(("yg", d)) for d in range(4)]
    bbs_b = [buf(("bbs", i)) for i in range(2)]
    cbs_b = [buf(("cbs", i)) for i in range(2)]
    hst_b = [buf(("hst", i)) for i in range(KI * DS)]
    gat_b = buf("gat")
    spa_b = buf("spa")
    srt_b = [buf(("srt", i)) for i in range(2)]
    pmm_b = [buf("p_mm0"), buf("p_mm1")]
    pax_b = [buf("p_ax0"), buf("p_ax1")]
    pbb_b, pcb_b = buf("p_bb"), buf("p_cb")
    pt32_b, pt16_b = buf("p_t32"), buf("p_t16")
    xn_b = buf("xn_t")
    x1n_b = buf("x1n_t")
    x1nt_b = buf("x1nt")
    sab_b = [buf(("sa", i)) for i in range(2)]
    tmp_b = [buf(("tmp", i)) for i in range(2)]
    sbb_b = [buf(("sbb", i)) for i in range(2)]
    wtp_b = [buf(("wtp", i)) for i in range(2)]
    sc_b = buf("sc")
    ew_b = buf("ew")
    ewt_b = buf("ewt")
    ewb_b = buf("ewb")
    tk_b = buf("tk")
    tk1_b = buf("tk1")
    wslot_b = [buf(("wsl", i)) for i in range(4)]
    lnst_b, lnmv_b, lnr_b = buf("lnst"), buf("lnmv"), buf("lnr")

    mxr_ps = [0]

    def mx_ps():
        k = mxr_ps[0] % 2
        mxr_ps[0] += 1
        return (p_t32, pt32_b) if k == 0 else (p_bb, pbb_b)

    ffr_ps = [0]

    def ff_ps():
        k = ffr_ps[0] % 2
        ffr_ps[0] += 1
        return p_ax[k], pax_b[k]

    slot_mx = [0]
    slot_ff = [0]

    def wslot():
        s_ = slot_mx[0] % 2
        slot_mx[0] += 1
        return t_w[:, s_ * 4096:(s_ + 1) * 4096], wslot_b[s_], f"ld_w{s_}"

    def wslot_f():
        s_ = 2 + slot_ff[0] % 2
        slot_ff[0] += 1
        return t_w[:, s_ * 4096:(s_ + 1) * 4096], wslot_b[s_], f"ld_w{s_}"

    def layer_norm(xt_ap, xt_bufs, out_ap, out_buf, tag):
        E("vector", lambda En, o=lnst[:, 0:6], i=xt_ap[:, 0:512]: En.bn_stats(o, i),
          xt_bufs, [lnst_b], tag=tag + "b0")
        E("vector", lambda En, o=lnst[:, 6:12], i=xt_ap[:, 512:1024]: En.bn_stats(o, i),
          xt_bufs, [lnst_b], tag=tag + "b1")
        E("vector", lambda En: En.bn_aggr(lnmv[:, :], lnst[:, 0:12]),
          [lnst_b], [lnmv_b], tag=tag + "ag")
        act(lnr[:, 0:1], lnmv[:, 1:2], AF.Ln, [lnmv_b], [lnr_b], bias=1e-5, tag=tag + "ln")
        act(lnr[:, 1:2], lnr[:, 0:1], AF.Exp, [lnr_b], [lnr_b], scale=-0.5, tag=tag + "ex")
        stt(lnr[:, 2:3], lnmv[:, 0:1], -1.0, lnr[:, 1:2], OP.mult, OP.mult,
            [lnmv_b, lnr_b], [lnr_b], tag=tag + "nm")
        ts("vector", out_ap, xt_ap, lnr[:, 1:2], lnr[:, 2:3], OP.mult, OP.add,
           xt_bufs + [lnr_b], [out_buf], tag=tag + "ap")

    xnt3 = xnt[:, :].rearrange("p (k t) -> p k t", k=KM)
    pt16_3 = p_t16[:, :].rearrange("p (k t) -> p k t", k=KM)
    x1nt3 = x1nt[:, :].rearrange("p (k t) -> p k t", k=KM)

    ystore_bufs = []

    def mixer_chunk(b, c):
        t0 = c * TC
        for j in range(4):
            xr = mx_roll[:, (j % 2) * 1024:(j % 2 + 1) * 1024]
            xrb = mxr_b[j % 2]
            dma("sync", xr, d_x[b, t0 + j * 128: t0 + (j + 1) * 128, :],
                [xrb], [xrb], "ld_x", tag=f"ldx{b}{c}{j}")
            xn_ap = xn_t[:, 0:1024]
            layer_norm(xr, [xrb], xn_ap, xn_b, f"l1_{b}{c}{j}")
            for i in range(KM):
                tp(p_t16[:, i * 128:(i + 1) * 128], xn_ap[:, i * 128:(i + 1) * 128],
                   ident_b[:, :], [xn_b, idb_b], [pt16_b], tag=f"txn{j}_{i}")
            cp("vector", xnt3[:, :, j * 128:(j + 1) * 128], pt16_3,
               [pt16_b], [xnt_b], tag=f"exn{j}")
            yield
        for mg in range(4):
            slv, slb, slsem = wslot()
            dma("sync", slv[:, 0:KM * 512].rearrange("p (k c) -> p k c", k=KM),
                s_win[:, mg * 512:(mg + 1) * 512].rearrange("(k p) c -> p k c", p=128),
                win_regs, [slb], slsem, tag=f"lwin{mg}")
            for ml in range(4):
                m = mg * 4 + ml
                pm, pb = mx_ps()
                for k in range(KM):
                    mm(pm[:, :], slv[:, k * 512 + ml * 128: k * 512 + (ml + 1) * 128],
                       xnt[:, k * TC:(k + 1) * TC], k == 0, k == KM - 1,
                       [slb, xnt_b], pb, tag=f"win{m}")
                act(t_su[:, m * TC:(m + 1) * TC], pm[:, :], AF.Silu, [pb], [tsu_b[m]],
                    tag=f"silu{m}")
                yield
        for wsb, wkey, dst, db in ((w_b_sb, "wbs", bmt, bmt_b),
                                   (w_c_sb, "wcs", cmt, cmt_b)):
            pm, pb = mx_ps()
            for k in range(KI):
                mm(pm[:, 0:TC], wsb[:, k * 128:(k + 1) * 128],
                   t_su[:, k * TC:(k + 1) * TC], k == 0, k == KI - 1,
                   [buf(wkey), tsu_b[k]], pb, tag=f"{wkey}{k}")
            act(dst[0:16, :], pm[0:16, 0:TC], AF.Copy, [pb], [db], tag=f"e{wkey}")
            yield
        for g in range(4):
            wd_sl = []
            for half in range(2):
                slv, slb, slsem = wslot()
                dma("sync", slv[:, 0:8 * 512].rearrange("p (k c) -> p k c", k=8),
                    s_wdl[half * 1024:(half + 1) * 1024, g * 512:(g + 1) * 512]
                    .rearrange("(k p) c -> p k c", p=128),
                    wdl_regs, [slb], slsem, tag=f"lwdl{g}{half}")
                wd_sl.append((slv, slb))
            for dt in range(4):
                m = g * 4 + dt
                pm, pb = mx_ps()
                for k in range(KI):
                    slv, slb = wd_sl[k // 8]
                    kk = k % 8
                    mm(pm[:, :], slv[:, kk * 512 + dt * 128: kk * 512 + (dt + 1) * 128],
                       t_su[:, k * TC:(k + 1) * TC], k == 0, k == KI - 1,
                       [slb, tsu_b[k]], pb, tag=f"wdl{m}")
                bcol = bdl_col[:, m:m + 1]
                dga = delta_g[:, dt * TC:(dt + 1) * TC]
                act(sp_a[:, :], pm[:, :], AF.Exp, [pb, buf("bdl")], [spa_b],
                    bias=bcol, tag=f"spE{m}")
                act(dga, sp_a[:, :], AF.Ln, [spa_b], [dg_b[dt]], bias=1.0, tag=f"spL{m}")
                tt("vector", OP.mult, du_g[:, dt * TC:(dt + 1) * TC], dga,
                   t_su[:, m * TC:(m + 1) * TC], [dg_b[dt], tsu_b[m]], [dug_b[dt]],
                   tag=f"du{m}")
                yield
            for n in range(DS):
                bsl = n % 2
                bba = bbs[:, bsl * TC:(bsl + 1) * TC]
                cba = cbs[:, bsl * TC:(bsl + 1) * TC]
                mm(p_bb[:, :], onehot[:, n * 128:(n + 1) * 128], bmt[:, :],
                   True, True, [oh_b, bmt_b], pbb_b, tag=f"bcb{g}{n}")
                act(bba, p_bb[:, :], AF.Copy, [pbb_b], [bbs_b[bsl]], tag=f"beb{g}{n}")
                mm(p_cb[:, :], onehot[:, n * 128:(n + 1) * 128], cmt[:, :],
                   True, True, [oh_b, cmt_b], pcb_b, tag=f"bcc{g}{n}")
                act(cba, p_cb[:, :], AF.Copy, [pcb_b], [cbs_b[bsl]], tag=f"bec{g}{n}")
                yield
                for dt in range(4):
                    m = g * 4 + dt
                    dsl = dt + 4 * (n % 2)
                    act(da_b[:, dsl * TC:(dsl + 1) * TC],
                        delta_g[:, dt * TC:(dt + 1) * TC], AF.Exp,
                        [dg_b[dt], acol_b], [da_bb[dsl]],
                        scale=a_col[:, m * DS + n: m * DS + n + 1], tag=f"dA{m}_{n}")
                bba4 = bass.AP(bbs, bsl * TC, [[2 * TC, 128], [0, 4], [1, TC]])
                tt("vector", OP.mult,
                   bin_b[:, :].rearrange("p (d t) -> p d t", d=4),
                   du_g[:, :].rearrange("p (d t) -> p d t", d=4), bba4,
                   [dug_b[0], dug_b[1], dug_b[2], dug_b[3], bbs_b[bsl]],
                   [bin_bb[0], bin_bb[1], bin_bb[2], bin_bb[3]], tag=f"bi{g}_{n}")
                yield
                for dt in range(4):
                    idx = (g * 4 + dt) * DS + n
                    dsl = dt + 4 * (n % 2)
                    scan(hseg_b[:, dt * TC:(dt + 1) * TC],
                         da_b[:, dsl * TC:(dsl + 1) * TC],
                         bin_b[:, dt * TC:(dt + 1) * TC], h_state[:, idx:idx + 1],
                         [da_bb[dsl], bin_bb[dt], hst_b[idx]], [hs_bb[dt]],
                         tag=f"sc{g}_{n}")
                yield
                for dt in range(4):
                    idx = (g * 4 + dt) * DS + n
                    cp("vector", h_state[:, idx:idx + 1],
                       hseg_b[:, (dt + 1) * TC - 1:(dt + 1) * TC],
                       [hs_bb[dt]], [hst_b[idx]], tag=f"cr{g}_{n}")
                cba4 = bass.AP(cbs, bsl * TC, [[2 * TC, 128], [0, 4], [1, TC]])
                tt("vector", OP.mult,
                   hp_b[:, :].rearrange("p (d t) -> p d t", d=4),
                   hseg_b[:, :].rearrange("p (d t) -> p d t", d=4), cba4,
                   [hs_bb[0], hs_bb[1], hs_bb[2], hs_bb[3], cbs_b[bsl]],
                   [hp_bb[0], hp_bb[1], hp_bb[2], hp_bb[3]], tag=f"hp{g}_{n}")
                yield
                for dt in range(4):
                    yga = y_g[:, dt * TC:(dt + 1) * TC]
                    hpa = hp_b[:, dt * TC:(dt + 1) * TC]
                    if n == 0:
                        cp("gpsimd", yga, hpa, [hp_bb[dt]], [yg_b[dt]], tag=f"y0{g}")
                    else:
                        tt("gpsimd", OP.add, yga, yga, hpa,
                           [hp_bb[dt], yg_b[dt]], [yg_b[dt]], tag=f"ya{g}_{n}")
                yield
            slv, slb, slsem = wslot()
            dma("sync", slv[:, 0:KM * 512].rearrange("p (k c) -> p k c", k=KM),
                s_win[:, DI + g * 512: DI + (g + 1) * 512].rearrange("(k p) c -> p k c", p=128),
                win_regs, [slb], slsem, tag=f"lwres{g}")
            for dt in range(4):
                m = g * 4 + dt
                pm, pb = mx_ps()
                for k in range(KM):
                    mm(pm[:, :], slv[:, k * 512 + dt * 128: k * 512 + (dt + 1) * 128],
                       xnt[:, k * TC:(k + 1) * TC], k == 0, k == KM - 1,
                       [slb, xnt_b], pb, tag=f"res{m}")
                sra = sres_t[:, (dt % 2) * TC:(dt % 2 + 1) * TC]
                act(sra, pm[:, :], AF.Silu, [pb], [srt_b[dt % 2]], tag=f"sres{m}")
                sua = t_su[:, m * TC:(m + 1) * TC]
                stt(gat_t[:, :], sua, d_colv[:, m:m + 1], y_g[:, dt * TC:(dt + 1) * TC],
                    OP.mult, OP.add, [tsu_b[m], buf("dcol"), yg_b[dt]], [gat_b],
                    tag=f"g1{m}")
                tt("vector", OP.mult, sua, gat_t[:, :], sra,
                   [gat_b, srt_b[dt % 2]], [tsu_b[m]], tag=f"g2{m}")
                yield
        for cg in range(2):
            wo_sl = []
            for half in range(2):
                slv, slb, slsem = wslot()
                dma("sync", slv[:, 0:8 * 512].rearrange("p (k c) -> p k c", k=8),
                    s_wout[half * 1024:(half + 1) * 1024, cg * 512:(cg + 1) * 512]
                    .rearrange("(k p) c -> p k c", p=128),
                    wout_regs, [slb], slsem, tag=f"lwo{cg}{half}")
                wo_sl.append((slv, slb))
            for ml in range(4):
                m = cg * 4 + ml
                pm, pb = mx_ps()
                for k in range(KI):
                    slv, slb = wo_sl[k // 8]
                    kk = k % 8
                    mm(pm[:, :], slv[:, kk * 512 + ml * 128: kk * 512 + (ml + 1) * 128],
                       t_su[:, k * TC:(k + 1) * TC], k == 0, k == KI - 1,
                       [slb, tsu_b[k]], pb, tag=f"wo{m}")
                act(ymix_t[:, m * TC:(m + 1) * TC], pm[:, :], AF.Copy,
                    [pb], [ymix_b[m]], tag=f"ewo{m}")
                yield
        for j in range(4):
            xr = mx_roll[:, (j % 2) * 1024:(j % 2 + 1) * 1024]
            xrb = mxr_b[j % 2]
            dma("sync", xr, d_x[b, t0 + j * 128: t0 + (j + 1) * 128, :],
                [xrb], [xrb], "ld_x", tag=f"rlx{b}{c}{j}")
            for i in range(KM):
                tp(p_t16[:, i * 128:(i + 1) * 128],
                   ymix_t[:, i * TC + j * 128: i * TC + (j + 1) * 128],
                   ident_b[:, :], [ymix_b[i], idb_b], [pt16_b], tag=f"tym{j}{i}")
            tt("vector", OP.add, xr, xr, p_t16[:, :],
               [xrb, pt16_b], [xrb], tag=f"res{j}")
            dma("scalar", s_x1[b, t0 + j * 128: t0 + (j + 1) * 128, :], xr,
                [xrb], [buf(("x1s", b, c))], "st_x1", tag=f"sx1{b}{c}{j}")
            xn_ap = x1n_t[:, 0:1024]
            layer_norm(xr, [xrb], xn_ap, x1n_b, f"l2_{b}{c}{j}")
            for i in range(KM):
                tp(p_t16[:, i * 128:(i + 1) * 128], xn_ap[:, i * 128:(i + 1) * 128],
                   ident_b[:, :], [x1n_b, idb_b], [pt16_b], tag=f"tx1n{j}_{i}")
            cp("vector", x1nt3[:, :, j * 128:(j + 1) * 128], pt16_3,
               [pt16_b], [x1nt_b], tag=f"ex1n{j}")
            yield
        dma("scalar", s_x1nt[b].rearrange("(k p) t -> p k t", p=128)[:, :, t0:t0 + TC],
            x1nt3, [x1nt_b], [buf(("x1nts", b, c))], "st_x1nt", tag=f"sx1n{b}{c}")

    def ffn_group(b, q):
        t0 = q * TC
        dma("sync", t_b2[:, :].rearrange("p (k t) -> p k t", k=KM),
            s_x1nt[b].rearrange("(k p) t -> p k t", p=128)[:, :, t0:t0 + TC],
            [buf(("x1nts", b, q))] + tb2_b, tb2_b, "ld_b2", tag=f"lfx{b}{q}")
        for tt_ in range(4):
            pm, pb = ff_ps()
            for k in range(KM):
                mm(pm[:, 0:NE], t_b2[:, k * TC + tt_ * 128: k * TC + (tt_ + 1) * 128],
                   wr_sb[:, k * NE:(k + 1) * NE], k == 0, k == KM - 1,
                   [tb2_b[k], buf("wrT")], pb, tag=f"sco{tt_}")
            sca = sc_sb[:, tt_ * NE:(tt_ + 1) * NE]
            tt("vector", OP.add, sca, pm[:, 0:NE], br_bc[:, :],
               [pb, buf("brbc")], [sc_b], tag=f"sce{tt_}")
            red("max", tk1[:, 0:1], sca, [sc_b], [tk1_b], tag="smx")
            ts("vector", tk1[:, 7:8], tk1[:, 0:1], -1.0, None, OP.mult, None,
               [tk1_b], [tk1_b], tag="snm")
            act(tk_sb[:, 0:8], sca, AF.Exp, [sc_b, tk1_b], [tk_b],
                bias=tk1[:, 7:8], tag="sex")
            red("sum", tk1[:, 1:2], tk_sb[:, 0:8], [tk_b], [tk1_b], tag="ssm")
            E("vector", lambda En: En.reciprocal(tk1[:, 2:3], tk1[:, 1:2]),
              [tk1_b], [tk1_b], tag="srp")
            ts("vector", tk_sb[:, 0:8], tk_sb[:, 0:8], tk1[:, 2:3], None, OP.mult, None,
               [tk_b, tk1_b], [tk_b], tag="sw")
            tt("vector", OP.add, tk_sb[:, 8:16], sca, bn_bc[:, :],
               [sc_b, buf("bnbc")], [tk_b], tag="sp1")
            red("max", tk1[:, 3:4], tk_sb[:, 8:16], [tk_b], [tk1_b], tag="sm1")
            ts("vector", tk_sb[:, 32:40], tk_sb[:, 8:16], tk1[:, 3:4], None, OP.is_ge, None,
               [tk_b, tk1_b], [tk_b], tag="sk1")
            stt(tk_sb[:, 16:24], tk_sb[:, 32:40], -1e30, tk_sb[:, 8:16],
                OP.mult, OP.add, [tk_b], [tk_b], tag="sms")
            red("max", tk1[:, 4:5], tk_sb[:, 16:24], [tk_b], [tk1_b], tag="sm2")
            ts("vector", tk_sb[:, 40:48], tk_sb[:, 16:24], tk1[:, 4:5], None, OP.is_ge, None,
               [tk_b, tk1_b], [tk_b], tag="sk2")
            tt("vector", OP.add, tk_sb[:, 32:40], tk_sb[:, 32:40], tk_sb[:, 40:48],
               [tk_b], [tk_b], tag="skm")
            tt("vector", OP.mult, tk_sb[:, 24:32], tk_sb[:, 0:8], tk_sb[:, 32:40],
               [tk_b], [tk_b], tag="svl")
            red("sum", tk1[:, 5:6], tk_sb[:, 24:32], [tk_b], [tk1_b], tag="sdn")
            E("vector", lambda En: En.reciprocal(tk1[:, 6:7], tk1[:, 5:6]),
              [tk1_b], [tk1_b], tag="sdr")
            ts("vector", ew_sb[:, tt_ * NE:(tt_ + 1) * NE], tk_sb[:, 24:32],
               tk1[:, 6:7], None, OP.mult, None, [tk_b, tk1_b], [ew_b], tag=f"sew{tt_}")
            yield
            pm2, pb2 = ff_ps()
            tp(pm2[0:8, 0:128], ew_sb[:, tt_ * NE:(tt_ + 1) * NE], ident_f[:, :],
               [ew_b, idf_b], [pb2], tag=f"ewt{tt_}")
            cp("vector", ewt_sb[0:8, tt_ * 128:(tt_ + 1) * 128], pm2[0:8, 0:128],
               [pb2], [ewt_b], tag=f"ewte{tt_}")
            yield
        for e in range(NE):
            mm(p_mm[0][:, :], onehot[:, e * 128:(e + 1) * 128], ewt_sb[:, :],
               True, True, [oh_b, ewt_b], pmm_b[0], tag=f"ewb{e}")
            act(ewb_sb[:, :], p_mm[0][:, :], AF.Copy, [pmm_b[0]], [ewb_b], tag=f"ewbe{e}")
            yield
            for band in range(4):
                slvA, slbA, slsemA = wslot_f()
                dma("sync", slvA[:, 0:KM * 512].rearrange("p (k c) -> p k c", k=KM),
                    s_wl1[e][:, band * 512:(band + 1) * 512]
                    .rearrange("(k p) c -> p k c", p=128),
                    wl1_regs[e], [slbA], slsemA, tag=f"lA{e}{band}")
                slvB, slbB, slsemB = wslot_f()
                dma("sync", slvB[:, 0:KM * 512].rearrange("p (k c) -> p k c", k=KM),
                    s_wl1[e][:, H + band * 512: H + (band + 1) * 512]
                    .rearrange("(k p) c -> p k c", p=128),
                    wl1_regs[e], [slbB], slsemB, tag=f"lB{e}{band}")
                for ml in range(4):
                    m = band * 4 + ml
                    pa, pab = p_mm[m % 2], pmm_b[m % 2]
                    pbv, pbb2 = p_ax[m % 2], pax_b[m % 2]
                    for k in range(KM):
                        mm(pa[:, :], slvA[:, k * 512 + ml * 128: k * 512 + (ml + 1) * 128],
                           t_b2[:, k * TC:(k + 1) * TC], k == 0, k == KM - 1,
                           [slbA, tb2_b[k]], pab, tag=f"h1a{e}{m}")
                    for k in range(KM):
                        mm(pbv[:, :], slvB[:, k * 512 + ml * 128: k * 512 + (ml + 1) * 128],
                           t_b2[:, k * TC:(k + 1) * TC], k == 0, k == KM - 1,
                           [slbB, tb2_b[k]], pbb2, tag=f"h1b{e}{m}")
                    sl = m % 2
                    saa = sa_b[:, sl * TC:(sl + 1) * TC]
                    act(saa, pa[:, :], AF.Silu, [pab, buf("bl1")],
                        [sab_b[sl]], bias=bl1_col[:, e * MH + m: e * MH + m + 1],
                        tag=f"sl{e}{m}")
                    stt(t_g[:, m * TC:(m + 1) * TC], pbv[:, :],
                        bl1_col[:, e * MH + 16 + m: e * MH + 17 + m], saa,
                        OP.add, OP.mult, [pbb2, buf("bl1"), sab_b[sl]],
                        [tg_b[m]], tag=f"gl{e}{m}")
                    yield
            for mp in range(4):
                slv, slb, slsem = wslot_f()
                dma("sync", slv[:, 0:KI * 256].rearrange("p (k c) -> p k c", k=KI),
                    s_wl2[e][:, mp * 256:(mp + 1) * 256]
                    .rearrange("(k p) c -> p k c", p=128),
                    wl2_regs[e], [slb], slsem, tag=f"lW2{e}{mp}")
                for m2l in range(2):
                    m2 = mp * 2 + m2l
                    pm, pb = p_mm[m2 % 2], pmm_b[m2 % 2]
                    for k in range(KI):
                        mm(pm[:, :], slv[:, k * 256 + m2l * 128: k * 256 + (m2l + 1) * 128],
                           t_g[:, k * TC:(k + 1) * TC], k == 0, k == KI - 1,
                           [slb, tg_b[k]], pb, tag=f"w2{e}{m2}")
                    ya = y_acct[:, m2 * TC:(m2 + 1) * TC]
                    sl = m2 % 2
                    if e == 0:
                        stt(ya, pm[:, :], bl2_col[:, e * KM + m2: e * KM + m2 + 1],
                            ewb_sb[:, :], OP.add, OP.mult,
                            [pb, buf("bl2"), ewb_b], [yat_b[m2]], tag=f"yi{e}{m2}")
                    else:
                        tmpa = tmp_s[:, sl * TC:(sl + 1) * TC]
                        stt(tmpa, pm[:, :], bl2_col[:, e * KM + m2: e * KM + m2 + 1],
                            ewb_sb[:, :], OP.add, OP.mult,
                            [pb, buf("bl2"), ewb_b], [tmp_b[sl]], tag=f"ym{e}{m2}")
                        tt("gpsimd", OP.add, ya, ya, tmpa,
                           [tmp_b[sl], yat_b[m2]], [yat_b[m2]], tag=f"yA{e}{m2}")
                    yield
        for j in range(4):
            xr = fx_roll[:, (j % 2) * 1024:(j % 2 + 1) * 1024]
            xrb = fxr_b[j % 2]
            dma("sync", xr, s_x1[b, t0 + j * 128: t0 + (j + 1) * 128, :],
                [buf(("x1s", b, q)), xrb], [xrb], "ld_b2", tag=f"lX1{b}{q}{j}")
            for half in range(2):
                pm, pb = ff_ps()
                for i in range(4):
                    mdm = half * 4 + i
                    tp(pm[:, i * 128:(i + 1) * 128],
                       y_acct[:, mdm * TC + j * 128: mdm * TC + (j + 1) * 128],
                       ident_f[:, :], [yat_b[mdm], idf_b], [pb], tag=f"tya{j}{half}")
                xh = xr[:, half * 512:(half + 1) * 512]
                tt("vector", OP.add, xh, xh, pm[:, :],
                   [xrb, pb], [xrb], tag=f"fo{j}{half}")
            yb = buf(("ys", b, q, j))
            dma("scalar", d_y[b, t0 + j * 128: t0 + (j + 1) * 128, :], xr,
                [xrb], [yb], "st_y", tag=f"sy{b}{q}{j}")
            ystore_bufs.append(yb)
            yield

    pend = []
    ffg = [None]

    def ffn_step(k):
        for _ in range(k):
            if ffg[0] is None:
                if not pend:
                    return
                ffg[0] = ffn_group(*pend.pop(0))
            try:
                next(ffg[0])
            except StopIteration:
                ffg[0] = None

    for b in range(BPC):
        mset(h_state[:, :], 0.0, hst_b, tag=f"hz{b}")
        for c in range(NCH):
            mg = mixer_chunk(b, c)
            INTERLEAVE = 1
            alive = True
            while alive:
                for _ in range(5):
                    try:
                        next(mg)
                    except StopIteration:
                        alive = False
                        break
                if INTERLEAVE:
                    ffn_step(2)
            if not INTERLEAVE:
                ffn_step(10 ** 6)
            pend.append((b, c))
    while pend or ffg[0] is not None:
        ffn_step(8)

    E("sync", lambda En: En.nop(), ystore_bufs, [], tag="final_wait")

    streams, tok, dcnt, ecnt = S.build_streams()
    dma_sem_map = {k: sems[k] for k in sems if k.startswith(("ld_", "st_"))}
    eng_sem_map = {e: sems[k] for e, k in
                   (("tensor", "pe"), ("vector", "v"), ("scalar", "s"),
                    ("gpsimd", "g"), ("sync", "pe"))}

    with es:
        with nc.Block() as blk0:
            @blk0.sync
            def _(sy):
                for s_ in sems.values():
                    sy.sem_clear(s_)

        with nc.Block() as blk:
            def run_stream(eng_name):
                def body(En):
                    ops = S.ops
                    for cmd in streams[eng_name]:
                        if cmd[0] == "wait":
                            (kind, key), val = cmd[1], cmd[2]
                            sem = dma_sem_map[key] if kind == "d" else eng_sem_map[key]
                            En.wait_ge(sem, val)
                        elif cmd[0] == "drain":
                            En.drain()
                        else:
                            i = cmd[1]
                            inst = ops[i][2](En)
                            tk = tok[i]
                            if tk is not None:
                                if tk[0] == "d":
                                    inst.then_inc(dma_sem_map[tk[1]], 16)
                                else:
                                    inst.then_inc(eng_sem_map[tk[1]], 1)
                return body

            blk.sync(run_stream("sync"))
            blk.tensor(run_stream("tensor"))
            blk.vector(run_stream("vector"))
            blk.scalar(run_stream("scalar"))
            blk.gpsimd(run_stream("gpsimd"))

    return nc


_CACHE = {}


def kernel(**inputs):
    if "nc" not in _CACHE:
        _CACHE["nc"] = build_nc()
    nc = _CACHE["nc"]
    key_map = {
        "x": "L0_attn_x", "w_in": "L0_attn_W_in", "w_delta": "L0_attn_W_delta",
        "b_delta": "L0_attn_b_delta", "w_b": "L0_attn_W_B", "w_c": "L0_attn_W_C",
        "a_log": "L0_attn_A_log", "d_param": "L0_attn_D_param",
        "w_out": "L0_attn_W_out", "wr": "L0_ffn_Wr", "br": "L0_ffn_br",
        "wl1": "L0_ffn_Wl1", "bl1": "L0_ffn_bl1", "wl2": "L0_ffn_Wl2",
        "bl2": "L0_ffn_bl2", "biases_n": "L0_ffn_biases_N",
    }
    full = {k: np.ascontiguousarray(np.asarray(inputs[v], dtype=np.float32))
            for k, v in key_map.items()}
    in_maps = []
    for c in range(NCORES):
        m = dict(full)
        m["x"] = np.ascontiguousarray(full["x"][c * BPC:(c + 1) * BPC])
        in_maps.append(m)
    res = run_bass_kernel_spmd(nc, in_maps, core_ids=list(range(NCORES)))
    out = np.concatenate([np.asarray(res.results[i]["y"]) for i in range(NCORES)], axis=0)
    return out.astype(np.float32, copy=False)


# version 19 (final)
# speedup vs baseline: 1.0612x; 1.0280x over previous
import contextlib
import numpy as np
import ml_dtypes
import concourse.bass as bass
import concourse.mybir as mybir
from concourse.bass_utils import run_bass_kernel_spmd
from concourse.alu_op_type import AluOpType

F32 = mybir.dt.float32
BF16 = mybir.dt.bfloat16
F16 = mybir.dt.float16
AF = mybir.ActivationFunctionType
AX = mybir.AxisListType
OP = AluOpType

NCORES = 8
BPC = 4
T = 2048
TC = 512
NCH = T // TC
DM = 1024
KM = DM // 128
DI = 2048
KI = DI // 128
DS = 16
NE = 8
H = 2048
MH = 32

ENGS = ("sync", "tensor", "vector", "scalar", "gpsimd")


class Buf:
    __slots__ = ("writer", "readers")

    def __init__(self):
        self.writer = None
        self.readers = []


class Sched:
    def __init__(self):
        self.ops = []
        self.last_dma = {}

    def op(self, eng, fn, reads=(), writes=(), dma=None, tag=""):
        i = len(self.ops)
        deps = set()
        if dma is not None and dma in self.last_dma:
            deps.add(self.last_dma[dma])
        for b in reads:
            if b.writer is not None:
                deps.add(b.writer)
        for b in writes:
            if b.writer is not None:
                deps.add(b.writer)
            deps.update(b.readers)
        deps.discard(i)
        self.ops.append([i, eng, fn, deps, dma, tag])
        if dma is not None:
            self.last_dma[dma] = i
        for b in reads:
            b.readers.append(i)
        for b in writes:
            b.writer = i
            b.readers = []
        return i

    def build_streams(self):
        ops = self.ops
        n = len(ops)
        has_cross = [False] * n
        for i, eng, fn, deps, dma, tag in ops:
            for d in deps:
                if ops[d][1] != eng or eng != "tensor":
                    has_cross[d] = True
        ecnt = {e: 0 for e in ENGS}
        dcnt = {}
        tok = [None] * n
        for i, eng, fn, deps, dma, tag in ops:
            if dma is not None:
                dcnt[dma] = dcnt.get(dma, 0) + 16
                tok[i] = ("d", dma, dcnt[dma])
            elif has_cross[i]:
                ecnt[eng] += 1
                tok[i] = ("e", eng, ecnt[eng])
        streams = {e: [] for e in ENGS}
        water = {e: {} for e in ENGS}
        drained_upto = {e: -1 for e in ENGS}
        last_own = {e: -1 for e in ENGS}
        run_dcnt = {}
        for i, eng, fn, deps, dma, tag in ops:
            st = streams[eng]
            need_drain = False
            for d in sorted(deps):
                deng, ddma = ops[d][1], ops[d][4]
                if ddma is not None:
                    val = run_dcnt.get(ddma, 0)
                    assert val >= tok[d][2], f"dma dep {d} after waiter {i}?"
                    key = ("d", ddma)
                    if water[eng].get(key, -1) < val:
                        water[eng][key] = val
                        st.append(("wait", key, val))
                elif deng != eng or eng != "tensor":
                    tk = tok[d]
                    assert tk is not None, f"dep {d} ({ops[d][5]}) of {i} ({tag}) untokenized"
                    key = (tk[0], tk[1])
                    if water[eng].get(key, -1) < tk[2]:
                        water[eng][key] = tk[2]
                        st.append(("wait", key, tk[2]))
                else:
                    pass
            if need_drain:
                st.append(("drain",))
                drained_upto[eng] = last_own[eng]
            st.append(("op", i))
            if dma is not None:
                run_dcnt[dma] = run_dcnt.get(dma, 0) + 16
            else:
                last_own[eng] = i
        return streams, tok, dcnt, ecnt


def build_nc():
    nc = bass.Bass("TRN2", target_bir_lowering=False)

    d_x = nc.dram_tensor("x", [BPC, T, DM], F32, kind="ExternalInput").ap()
    d_win = nc.dram_tensor("w_in", [DM, 2 * DI], F32, kind="ExternalInput").ap()
    d_wdl = nc.dram_tensor("w_delta", [DI, DI], F32, kind="ExternalInput").ap()
    d_bdl = nc.dram_tensor("b_delta", [DI], F32, kind="ExternalInput").ap()
    d_wb = nc.dram_tensor("w_b", [DI, DS], F32, kind="ExternalInput").ap()
    d_wc = nc.dram_tensor("w_c", [DI, DS], F32, kind="ExternalInput").ap()
    d_alog = nc.dram_tensor("a_log", [DI, DS], F32, kind="ExternalInput").ap()
    d_dpar = nc.dram_tensor("d_param", [DI], F32, kind="ExternalInput").ap()
    d_wout = nc.dram_tensor("w_out", [DI, DM], F32, kind="ExternalInput").ap()
    d_wr = nc.dram_tensor("wr", [NE, DM], F32, kind="ExternalInput").ap()
    d_br = nc.dram_tensor("br", [NE], F32, kind="ExternalInput").ap()
    d_wl1 = nc.dram_tensor("wl1", [NE, DM, 2 * H], F32, kind="ExternalInput").ap()
    d_bl1 = nc.dram_tensor("bl1", [NE, 2 * H], F32, kind="ExternalInput").ap()
    d_wl2 = nc.dram_tensor("wl2", [NE, H, DM], F32, kind="ExternalInput").ap()
    d_bl2 = nc.dram_tensor("bl2", [NE, DM], F32, kind="ExternalInput").ap()
    d_bn = nc.dram_tensor("biases_n", [NE], F32, kind="ExternalInput").ap()
    d_y = nc.dram_tensor("y", [BPC, T, DM], F32, kind="ExternalOutput").ap()

    s_win = nc.dram_tensor("s_win", [DM, 2 * DI], BF16).ap()
    s_wdl = nc.dram_tensor("s_wdl", [DI, DI], BF16).ap()
    s_wout = nc.dram_tensor("s_wout", [DI, DM], BF16).ap()
    s_wl1 = nc.dram_tensor("s_wl1", [NE, DM, 2 * H], BF16).ap()
    s_wl2 = nc.dram_tensor("s_wl2", [NE, H, DM], BF16).ap()
    s_x1 = nc.dram_tensor("s_x1", [BPC, T, DM], F32).ap()
    s_x1nt = nc.dram_tensor("s_x1nt", [BPC, DM, T], BF16).ap()

    es = contextlib.ExitStack()

    def sbuf(name, p, f, dt):
        return es.enter_context(nc.sbuf_tensor(name, [p, f], dt))

    def psum(name, p, f, dt):
        return es.enter_context(nc.psum_tensor(name, [p, f], dt))

    cpool = sbuf("cpool", 128, 8, F32)
    ident_b = sbuf("ident_b", 128, 128, BF16)
    ident_f = sbuf("ident_f", 128, 128, F32)
    onehot = sbuf("onehot", 128, 16 * 128, F16)
    a_col = sbuf("a_col", 128, KI * DS, F32)
    d_colv = sbuf("d_colv", 128, KI, F32)
    bdl_col = sbuf("bdl_col", 128, KI, F32)
    bl1_col = sbuf("bl1_col", 128, NE * MH, F32)
    bl2_col = sbuf("bl2_col", 128, NE * KM, F32)
    br_row = sbuf("br_row", 1, NE, F32)
    bn_row = sbuf("bn_row", 1, NE, F32)
    ones_r = sbuf("ones_r", 1, 128, F32)
    br_bc = sbuf("br_bc", 128, NE, F32)
    bn_bc = sbuf("bn_bc", 128, NE, F32)
    wr_sb = sbuf("wr_sb", 128, KM * NE, BF16)
    wr_tmp = sbuf("wr_tmp", 8, DM, BF16)
    w_b_sb = sbuf("w_b_sb", 128, KI * 128, BF16)
    w_c_sb = sbuf("w_c_sb", 128, KI * 128, BF16)
    h_state = sbuf("h_state", 128, KI * DS, F32)

    mx_roll = sbuf("mx_roll", 128, 2048, F32)
    xn_t = sbuf("xn_t", 128, 1024, BF16)
    xnt = sbuf("xnt", 128, KM * TC, BF16)
    t_su = sbuf("t_su", 128, KI * TC, BF16)
    delta_g = sbuf("delta_g", 128, 4 * TC, F16)
    du_g = sbuf("du_g", 128, 4 * TC, F16)
    da_b = sbuf("da_b", 128, 8 * TC, F16)
    bin_b = sbuf("bin_b", 128, 4 * TC, F16)
    hseg_b = sbuf("hseg_b", 128, 4 * TC, F16)
    hp_b = sbuf("hp_b", 128, 4 * TC, F16)
    bbs = sbuf("bbs", 128, 2 * TC, F16)
    cbs = sbuf("cbs", 128, 2 * TC, F16)
    y_g = sbuf("y_g", 128, 4 * TC, F16)
    gat_t = sbuf("gat_t", 128, TC, F16)
    sp_a = sbuf("sp_a", 128, TC, F16)
    sres_t = sbuf("sres_t", 128, 2 * TC, BF16)
    bmt = sbuf("bmt", 128, TC, F16)
    cmt = sbuf("cmt", 128, TC, F16)
    x1n_t = sbuf("x1n_t", 128, 1024, BF16)
    x1nt = sbuf("x1nt", 128, KM * TC, BF16)
    ymix_t = sbuf("ymix_t", 128, KM * TC, BF16)
    lnst = sbuf("lnst", 128, 12, F32)
    lnmv = sbuf("lnmv", 128, 2, F32)
    lnr = sbuf("lnr", 128, 4, F32)
    fx_roll = sbuf("fx_roll", 128, 2048, F32)
    t_b2 = sbuf("t_b2", 128, KM * TC, BF16)
    t_g = sbuf("t_g", 128, KI * TC, BF16)
    y_acct = sbuf("y_acct", 128, KM * TC, F32)
    sa_b = sbuf("sa_b", 128, 2 * TC, BF16)
    sb_b2 = sbuf("sb_b2", 128, 2 * TC, BF16)
    wtmp = sbuf("wtmp", 128, 2 * TC, BF16)
    tmp_s = sbuf("tmp_s", 128, 2 * TC, F16)
    ewt_sb = sbuf("ewt_sb", 128, TC, F16)
    ewb_sb = sbuf("ewb_sb", 128, TC, F16)
    sc_sb = sbuf("sc_sb", 128, 4 * NE, F32)
    ew_sb = sbuf("ew_sb", 128, 4 * NE, F32)
    tk_sb = sbuf("tk_sb", 128, 6 * NE, F32)
    tk1 = sbuf("tk1", 128, 8, F32)
    t_w = sbuf("t_w", 128, 4 * 4096, BF16)

    p_mm = [psum("p_mm0", 128, 512, F32), psum("p_mm1", 128, 512, F32)]
    p_ax = [psum("p_ax0", 128, 512, F32), psum("p_ax1", 128, 512, F32)]
    p_bb = psum("p_bb", 128, 512, F32)
    p_cb = psum("p_cb", 128, 512, F32)
    p_t32 = psum("p_t32", 128, 512, F32)
    p_t16 = psum("p_t16", 128, 1024, BF16)

    sems = {}
    for name in ("pe", "v", "s", "g",
                 "ld_x", "ld_b2", "ld_w0", "ld_w1", "ld_w2", "ld_w3",
                 "st_win", "st_wdl", "st_wout", "st_wl1", "st_wl2",
                 "st_c0", "st_c1", "st_c2",
                 "st_x1", "st_x1nt", "st_y"):
        sems[name] = es.enter_context(nc.semaphore("sem_" + name))

    S = Sched()
    B = {}

    def buf(key):
        if key not in B:
            B[key] = Buf()
        return B[key]

    def E(eng, fn, reads=(), writes=(), dma=None, tag=""):
        return S.op(eng, fn, reads, writes, dma, tag)

    def mm(out_ap, lhs_ap, rhs_ap, start, stop, reads, wbuf, tag=""):
        E("tensor", lambda En, o=out_ap, l=lhs_ap, r=rhs_ap, a=start, z=stop:
          En.matmul(o, l, r, start=a, stop=z), reads, [wbuf], tag=tag)

    def tp(out_ap, in_ap, id_ap, reads, writes, tag=""):
        E("tensor", lambda En, o=out_ap, i=in_ap, d=id_ap: En.transpose(o, i, d),
          reads, writes, tag=tag)

    def tt(eng, alu, out_ap, a_ap, b_ap, reads, writes, tag=""):
        E(eng, lambda En, o=out_ap, a=a_ap, b=b_ap, u=alu: En.tensor_tensor(o, a, b, u),
          reads, writes, tag=tag)

    def ts(eng, out_ap, in_ap, s1, s2, op0, op1, reads, writes, tag=""):
        if op1 is None:
            E(eng, lambda En, o=out_ap, i=in_ap, a=s1, u=op0: En.tensor_scalar(o, i, a, None, u),
              reads, writes, tag=tag)
        else:
            E(eng, lambda En, o=out_ap, i=in_ap, a=s1, b=s2, u=op0, w=op1:
              En.tensor_scalar(o, i, a, b, u, w), reads, writes, tag=tag)

    def stt(out_ap, in0_ap, sc, in1_ap, op0, op1, reads, writes, tag=""):
        E("vector", lambda En, o=out_ap, i=in0_ap, s=sc, j=in1_ap, u=op0, w=op1:
          En.scalar_tensor_tensor(o, i, s, j, u, w), reads, writes, tag=tag)

    def act(out_ap, in_ap, func, reads, writes, bias=None, scale=None, tag=""):
        kw = {}
        if bias is not None:
            kw["bias"] = bias
        if scale is not None:
            kw["scale"] = scale
        E("scalar", lambda En, o=out_ap, i=in_ap, f=func, k=kw: En.activation(o, i, f, **k),
          reads, writes, tag=tag)

    def cp(eng, out_ap, in_ap, reads, writes, tag=""):
        E(eng, lambda En, o=out_ap, i=in_ap: En.tensor_copy(o, i), reads, writes, tag=tag)

    def red(kind, out_ap, in_ap, reads, writes, tag=""):
        if kind == "max":
            E("vector", lambda En, o=out_ap, i=in_ap: En.reduce_max(o, i, axis=AX.X),
              reads, writes, tag=tag)
        else:
            E("vector", lambda En, o=out_ap, i=in_ap: En.reduce_sum(o, i, axis=AX.X),
              reads, writes, tag=tag)

    def scan(out_ap, d0, d1, init_ap, reads, writes, tag=""):
        E("vector", lambda En, o=out_ap, a=d0, b=d1, i=init_ap:
          En.tensor_tensor_scan(o, a, b, i, OP.mult, OP.add), reads, writes, tag=tag)

    def dma(q, out_ap, in_ap, reads, writes, sem, tag=""):
        E(q, lambda En, o=out_ap, i=in_ap: En.dma_start(o, i), reads, writes, dma=sem, tag=tag)

    def mset(ap, val, writes, tag=""):
        E("gpsimd", lambda En, a=ap, v=val: En.memset(a, v), (), writes, tag=tag)

    nc.const_aps.register(F32, 1.0, cpool[:, 0:1])
    nc.const_aps.register(F32, -0.5, cpool[:, 1:2])
    nc.const_aps.register(F32, 1e-5, cpool[:, 2:3])
    nc.const_aps.register(F32, -1.0, cpool[:, 3:4])

    mxr_b = [buf(("mxr", i)) for i in range(2)]
    STG = mxr_b

    cp_b = buf("cpool")
    for j, val in enumerate([1.0, -0.5, 1e-5, -1.0]):
        mset(cpool[:, j:j + 1], val, [cp_b], tag="cpool")
    mset(ones_r[:, :], 1.0, [buf("ones")], tag="ones")
    idf_b, idb_b = buf("identf"), buf("identb")
    E("gpsimd", lambda En: En.iota(ident_f[:, :], [[1, 128]], channel_multiplier=-1,
                                   allow_small_or_imprecise_dtypes=True),
      (), [idf_b], tag="iota")
    ts("gpsimd", ident_f[:, :], ident_f[:, :], 0.0, None, OP.is_equal, None,
       [idf_b], [idf_b], tag="identf")
    cp("gpsimd", ident_b[:, :], ident_f[:, :], [idf_b], [idb_b], tag="identb")
    oh_b = buf("oh")
    E("gpsimd", lambda En: En.iota(mx_roll[0:128, 0:2048], [[1, 16], [0, 128]],
                                   channel_multiplier=-1,
                                   allow_small_or_imprecise_dtypes=True),
      (), STG, tag="ohiota")
    ts("gpsimd", onehot[:, :], mx_roll[0:128, 0:2048], 0.0, None, OP.is_equal, None,
       STG, [oh_b], tag="onehot")

    dma("sync", mx_roll[:, 0:KI * DS].rearrange("p (m n) -> p m n", m=KI),
        d_alog.rearrange("(m p) n -> p m n", p=128), STG, STG, "ld_x", tag="ld_alog")
    acol_b = buf("acol")
    act(a_col[:, :], mx_roll[:, 0:KI * DS], AF.Exp, STG, [acol_b], tag="a_exp")
    ts("vector", a_col[:, :], a_col[:, :], -1.0, None, OP.mult, None,
       [acol_b], [acol_b], tag="a_neg")
    pt32_b0 = buf("p_t32")
    idf_b0 = buf("identf")

    def vec_load(dst_ap, dst_key, src_ap, rows, tg):
        dma("sync", mx_roll[0:rows, 0:128], src_ap, STG, STG, "ld_x", tag=tg + "l")
        tp(p_t32[:, 0:rows], mx_roll[0:rows, 0:128], ident_f[0:rows, 0:rows],
           STG + [idf_b0], [pt32_b0], tag=tg + "t")
        cp("vector", dst_ap, p_t32[:, 0:rows], [pt32_b0], [buf(dst_key)], tag=tg + "e")

    vec_load(d_colv[:, :], "dcol", d_dpar.rearrange("(m p) -> m p", p=128), 16, "vdp")
    vec_load(bdl_col[:, :], "bdl", d_bdl.rearrange("(m p) -> m p", p=128), 16, "vbd")
    for e_ in range(NE):
        vec_load(bl1_col[:, e_ * MH:(e_ + 1) * MH], "bl1",
                 d_bl1[e_, :].rearrange("(m p) -> m p", p=128), 32, f"vb1{e_}")
        vec_load(bl2_col[:, e_ * KM:(e_ + 1) * KM], "bl2",
                 d_bl2[e_, :].rearrange("(m p) -> m p", p=128), 8, f"vb2{e_}")
    dma("sync", br_row[:, :], d_br.rearrange("(a n) -> a n", a=1), (), [buf("brrow")], "ld_x")
    dma("sync", bn_row[:, :], d_bn.rearrange("(a n) -> a n", a=1), (), [buf("bnrow")], "ld_x")
    mset(w_b_sb[:, :], 0.0, [buf("wbs")], tag="wbz")
    mset(w_c_sb[:, :], 0.0, [buf("wcs")], tag="wcz")
    mset(bmt[:, :], 0.0, [buf("bmt")], tag="bmz")
    mset(cmt[:, :], 0.0, [buf("cmt")], tag="cmz")
    mset(ewt_sb[:, :], 0.0, [buf("ewt")], tag="ewz")
    dma("sync", mx_roll[:, 0:KI * DS].rearrange("p (m n) -> p m n", m=KI),
        d_wb.rearrange("(m p) n -> p m n", p=128), STG, STG, "ld_x")
    cp("vector", w_b_sb[:, :].rearrange("p (k c) -> p k c", k=KI)[:, :, 0:DS],
       mx_roll[:, 0:KI * DS].rearrange("p (k c) -> p k c", k=KI), STG, [buf("wbs")], tag="wb_cast")
    dma("sync", mx_roll[:, 0:KI * DS].rearrange("p (m n) -> p m n", m=KI),
        d_wc.rearrange("(m p) n -> p m n", p=128), STG, STG, "ld_x")
    cp("vector", w_c_sb[:, :].rearrange("p (k c) -> p k c", k=KI)[:, :, 0:DS],
       mx_roll[:, 0:KI * DS].rearrange("p (k c) -> p k c", k=KI), STG, [buf("wcs")], tag="wc_cast")
    dma("sync", mx_roll[0:8, 0:DM], d_wr, STG, STG, "ld_x")
    cp("vector", wr_tmp[:, :], mx_roll[0:8, 0:DM], STG, [buf("wrb")], tag="wr_cast")

    for row_b, row, dst, dkey in ((buf("brrow"), br_row, br_bc, "brbc"),
                                  (buf("bnrow"), bn_row, bn_bc, "bnbc")):
        mm(p_ax[0][:, 0:NE], ones_r[0:1, :], row[0:1, :], True, True,
           [row_b, buf("ones")], buf("p_ax0"), tag="bc_" + dkey)
        cp("vector", dst[:, :], p_ax[0][:, 0:NE], [buf("p_ax0")], [buf(dkey)], tag="bce_" + dkey)

    for k in range(KM):
        tp(p_t16[:, 0:NE], wr_tmp[:, k * 128:(k + 1) * 128], ident_b[0:8, 0:8],
           [buf("wrb"), idb_b], [buf("p_t16")], tag=f"wr_tp{k}")
        cp("vector", wr_sb[:, k * NE:(k + 1) * NE], p_t16[:, 0:NE],
           [buf("p_t16")], [buf("wrT")], tag=f"wr_ev{k}")

    cast_jobs = []
    win_regs, wdl_regs, wout_regs = [], [], []
    wl1_regs = [[] for _ in range(NE)]
    wl2_regs = [[] for _ in range(NE)]
    for kt in range(KM):
        r = buf(("scr", "win", kt)); win_regs.append(r)
        for hh in range(2):
            cast_jobs.append((d_win[kt * 128:(kt + 1) * 128, hh * 2048:(hh + 1) * 2048],
                              s_win[kt * 128:(kt + 1) * 128, hh * 2048:(hh + 1) * 2048],
                              2048, r, "st_win"))
    for kt in range(KI):
        r = buf(("scr", "wdl", kt)); wdl_regs.append(r)
        cast_jobs.append((d_wdl[kt * 128:(kt + 1) * 128, :],
                          s_wdl[kt * 128:(kt + 1) * 128, :], 2048, r, "st_wdl"))
    for kt in range(KI):
        r = buf(("scr", "wout", kt)); wout_regs.append(r)
        cast_jobs.append((d_wout[kt * 128:(kt + 1) * 128, :],
                          s_wout[kt * 128:(kt + 1) * 128, :], 1024, r, "st_wout"))
    for e in range(NE):
        for kt in range(KM):
            r = buf(("scr", "wl1", e, kt)); wl1_regs[e].append(r)
            for hh in range(2):
                cast_jobs.append((d_wl1[e, kt * 128:(kt + 1) * 128, hh * 2048:(hh + 1) * 2048],
                                  s_wl1[e, kt * 128:(kt + 1) * 128, hh * 2048:(hh + 1) * 2048],
                                  2048, r, "st_wl1"))
        for kt in range(0, KI, 2):
            r = buf(("scr", "wl2", e, kt)); wl2_regs[e].append(r)
            cast_jobs.append((d_wl2[e, kt * 128:(kt + 2) * 128, :].rearrange(
                "(a p) m -> p a m", p=128),
                s_wl2[e, kt * 128:(kt + 2) * 128, :].rearrange("(a p) m -> p a m", p=128),
                2048, r, "st_wl2"))
    cast_engs = ("gpsimd", "scalar")
    cb_b = [buf(("wsl", i)) for i in range(4)]
    cfxr_b = [buf(("fxr", i)) for i in range(2)]
    cstg = [(mx_roll, STG, "ld_b2"), (fx_roll, cfxr_b, "ld_w3")]
    for ji, (csrc, cdst, cols, reg, stsem) in enumerate(cast_jobs):
        bs = ji % 4
        stg_t, stg_bufs, ldsem = cstg[ji % 2]
        fview = stg_t[:, 0:cols]
        bview = t_w[:, bs * 4096: bs * 4096 + cols]
        fv3 = fview.rearrange("p (a m) -> p a m", a=2) if (cols == 2048 and len(csrc.shape) == 3) else fview
        eng = cast_engs[ji % 2]
        dma("sync", fv3, csrc, stg_bufs, stg_bufs, ldsem, tag=f"cld{ji}")
        if eng == "scalar":
            act(bview, fview, AF.Copy, stg_bufs, [cb_b[bs]], tag=f"cst{ji}")
        else:
            cp(eng, bview, fview, stg_bufs, [cb_b[bs]], tag=f"cst{ji}")
        bv3 = bview.rearrange("p (a m) -> p a m", a=2) if (cols == 2048 and len(cdst.shape) == 3) else bview
        dma(eng, cdst, bv3, [cb_b[bs]], [reg], f"st_c{ji % 2}", tag=f"cso{ji}")

    fxr_b = [buf(("fxr", i)) for i in range(2)]
    xnt_b = buf("xnt")
    tsu_b = [buf(("su", m)) for m in range(KI)]
    tb2_b = [buf(("b2", m)) for m in range(KM)]
    tg_b = [buf(("tg", m)) for m in range(KI)]
    yat_b = [buf(("yat", m)) for m in range(KM)]
    ymix_b = [buf(("ym", m)) for m in range(KM)]
    bmt_b, cmt_b = buf("bmt"), buf("cmt")
    dg_b = [buf(("dg", d)) for d in range(4)]
    dug_b = [buf(("dug", d)) for d in range(4)]
    da_bb = [buf(("da", i)) for i in range(8)]
    bin_bb = [buf(("bin", i)) for i in range(4)]
    hs_bb = [buf(("hseg", i)) for i in range(4)]
    hp_bb = [buf(("hp", i)) for i in range(4)]
    yg_b = [buf(("yg", d)) for d in range(4)]
    bbs_b = [buf(("bbs", i)) for i in range(2)]
    cbs_b = [buf(("cbs", i)) for i in range(2)]
    hst_b = [buf(("hst", i)) for i in range(KI * DS)]
    gat_b = buf("gat")
    spa_b = buf("spa")
    srt_b = [buf(("srt", i)) for i in range(2)]
    pmm_b = [buf("p_mm0"), buf("p_mm1")]
    pax_b = [buf("p_ax0"), buf("p_ax1")]
    pbb_b, pcb_b = buf("p_bb"), buf("p_cb")
    pt32_b, pt16_b = buf("p_t32"), buf("p_t16")
    xn_b = buf("xn_t")
    x1n_b = buf("x1n_t")
    x1nt_b = buf("x1nt")
    sab_b = [buf(("sa", i)) for i in range(2)]
    tmp_b = [buf(("tmp", i)) for i in range(2)]
    sbb_b = [buf(("sbb", i)) for i in range(2)]
    wtp_b = [buf(("wtp", i)) for i in range(2)]
    sc_b = buf("sc")
    ew_b = buf("ew")
    ewt_b = buf("ewt")
    ewb_b = buf("ewb")
    tk_b = buf("tk")
    tk1_b = buf("tk1")
    wslot_b = [buf(("wsl", i)) for i in range(4)]
    lnst_b, lnmv_b, lnr_b = buf("lnst"), buf("lnmv"), buf("lnr")

    mxr_ps = [0]

    def mx_ps():
        k = mxr_ps[0] % 2
        mxr_ps[0] += 1
        return (p_t32, pt32_b) if k == 0 else (p_bb, pbb_b)

    ffr_ps = [0]

    def ff_ps():
        k = ffr_ps[0] % 2
        ffr_ps[0] += 1
        return p_ax[k], pax_b[k]

    slot_mx = [0]
    slot_ff = [0]

    def wslot():
        s_ = slot_mx[0] % 2
        slot_mx[0] += 1
        return t_w[:, s_ * 4096:(s_ + 1) * 4096], wslot_b[s_], f"ld_w{s_}"

    def wslot_f():
        s_ = 2 + slot_ff[0] % 2
        slot_ff[0] += 1
        return t_w[:, s_ * 4096:(s_ + 1) * 4096], wslot_b[s_], f"ld_w{s_}"

    def layer_norm(xt_ap, xt_bufs, out_ap, out_buf, tag):
        E("vector", lambda En, o=lnst[:, 0:6], i=xt_ap[:, 0:512]: En.bn_stats(o, i),
          xt_bufs, [lnst_b], tag=tag + "b0")
        E("vector", lambda En, o=lnst[:, 6:12], i=xt_ap[:, 512:1024]: En.bn_stats(o, i),
          xt_bufs, [lnst_b], tag=tag + "b1")
        E("vector", lambda En: En.bn_aggr(lnmv[:, :], lnst[:, 0:12]),
          [lnst_b], [lnmv_b], tag=tag + "ag")
        act(lnr[:, 0:1], lnmv[:, 1:2], AF.Ln, [lnmv_b], [lnr_b], bias=1e-5, tag=tag + "ln")
        act(lnr[:, 1:2], lnr[:, 0:1], AF.Exp, [lnr_b], [lnr_b], scale=-0.5, tag=tag + "ex")
        stt(lnr[:, 2:3], lnmv[:, 0:1], -1.0, lnr[:, 1:2], OP.mult, OP.mult,
            [lnmv_b, lnr_b], [lnr_b], tag=tag + "nm")
        ts("vector", out_ap, xt_ap, lnr[:, 1:2], lnr[:, 2:3], OP.mult, OP.add,
           xt_bufs + [lnr_b], [out_buf], tag=tag + "ap")

    xnt3 = xnt[:, :].rearrange("p (k t) -> p k t", k=KM)
    pt16_3 = p_t16[:, :].rearrange("p (k t) -> p k t", k=KM)
    x1nt3 = x1nt[:, :].rearrange("p (k t) -> p k t", k=KM)

    ystore_bufs = []

    def mixer_chunk(b, c):
        t0 = c * TC
        for j in range(4):
            xr = mx_roll[:, (j % 2) * 1024:(j % 2 + 1) * 1024]
            xrb = mxr_b[j % 2]
            dma("sync", xr, d_x[b, t0 + j * 128: t0 + (j + 1) * 128, :],
                [xrb], [xrb], "ld_x", tag=f"ldx{b}{c}{j}")
            xn_ap = xn_t[:, 0:1024]
            layer_norm(xr, [xrb], xn_ap, xn_b, f"l1_{b}{c}{j}")
            for i in range(KM):
                tp(p_t16[:, i * 128:(i + 1) * 128], xn_ap[:, i * 128:(i + 1) * 128],
                   ident_b[:, :], [xn_b, idb_b], [pt16_b], tag=f"txn{j}_{i}")
            cp("vector", xnt3[:, :, j * 128:(j + 1) * 128], pt16_3,
               [pt16_b], [xnt_b], tag=f"exn{j}")
            yield
        for mg in range(4):
            slv, slb, slsem = wslot()
            dma("sync", slv[:, 0:KM * 512].rearrange("p (k c) -> p k c", k=KM),
                s_win[:, mg * 512:(mg + 1) * 512].rearrange("(k p) c -> p k c", p=128),
                win_regs, [slb], slsem, tag=f"lwin{mg}")
            for ml in range(4):
                m = mg * 4 + ml
                pm, pb = mx_ps()
                for k in range(KM):
                    mm(pm[:, :], slv[:, k * 512 + ml * 128: k * 512 + (ml + 1) * 128],
                       xnt[:, k * TC:(k + 1) * TC], k == 0, k == KM - 1,
                       [slb, xnt_b], pb, tag=f"win{m}")
                act(t_su[:, m * TC:(m + 1) * TC], pm[:, :], AF.Silu, [pb], [tsu_b[m]],
                    tag=f"silu{m}")
                yield
        for wsb, wkey, dst, db in ((w_b_sb, "wbs", bmt, bmt_b),
                                   (w_c_sb, "wcs", cmt, cmt_b)):
            pm, pb = mx_ps()
            for k in range(KI):
                mm(pm[:, 0:TC], wsb[:, k * 128:(k + 1) * 128],
                   t_su[:, k * TC:(k + 1) * TC], k == 0, k == KI - 1,
                   [buf(wkey), tsu_b[k]], pb, tag=f"{wkey}{k}")
            act(dst[0:16, :], pm[0:16, 0:TC], AF.Copy, [pb], [db], tag=f"e{wkey}")
            yield
        for g in range(4):
            wd_sl = []
            for half in range(2):
                slv, slb, slsem = wslot()
                dma("sync", slv[:, 0:8 * 512].rearrange("p (k c) -> p k c", k=8),
                    s_wdl[half * 1024:(half + 1) * 1024, g * 512:(g + 1) * 512]
                    .rearrange("(k p) c -> p k c", p=128),
                    wdl_regs, [slb], slsem, tag=f"lwdl{g}{half}")
                wd_sl.append((slv, slb))
            for dt in range(4):
                m = g * 4 + dt
                pm, pb = mx_ps()
                for k in range(KI):
                    slv, slb = wd_sl[k // 8]
                    kk = k % 8
                    mm(pm[:, :], slv[:, kk * 512 + dt * 128: kk * 512 + (dt + 1) * 128],
                       t_su[:, k * TC:(k + 1) * TC], k == 0, k == KI - 1,
                       [slb, tsu_b[k]], pb, tag=f"wdl{m}")
                bcol = bdl_col[:, m:m + 1]
                dga = delta_g[:, dt * TC:(dt + 1) * TC]
                act(sp_a[:, :], pm[:, :], AF.Exp, [pb, buf("bdl")], [spa_b],
                    bias=bcol, tag=f"spE{m}")
                act(dga, sp_a[:, :], AF.Ln, [spa_b], [dg_b[dt]], bias=1.0, tag=f"spL{m}")
                tt("vector", OP.mult, du_g[:, dt * TC:(dt + 1) * TC], dga,
                   t_su[:, m * TC:(m + 1) * TC], [dg_b[dt], tsu_b[m]], [dug_b[dt]],
                   tag=f"du{m}")
                yield
            for n in range(DS):
                bsl = n % 2
                bba = bbs[:, bsl * TC:(bsl + 1) * TC]
                cba = cbs[:, bsl * TC:(bsl + 1) * TC]
                mm(p_bb[:, :], onehot[:, n * 128:(n + 1) * 128], bmt[:, :],
                   True, True, [oh_b, bmt_b], pbb_b, tag=f"bcb{g}{n}")
                act(bba, p_bb[:, :], AF.Copy, [pbb_b], [bbs_b[bsl]], tag=f"beb{g}{n}")
                mm(p_cb[:, :], onehot[:, n * 128:(n + 1) * 128], cmt[:, :],
                   True, True, [oh_b, cmt_b], pcb_b, tag=f"bcc{g}{n}")
                act(cba, p_cb[:, :], AF.Copy, [pcb_b], [cbs_b[bsl]], tag=f"bec{g}{n}")
                yield
                for dt in range(4):
                    m = g * 4 + dt
                    dsl = dt + 4 * (n % 2)
                    act(da_b[:, dsl * TC:(dsl + 1) * TC],
                        delta_g[:, dt * TC:(dt + 1) * TC], AF.Exp,
                        [dg_b[dt], acol_b], [da_bb[dsl]],
                        scale=a_col[:, m * DS + n: m * DS + n + 1], tag=f"dA{m}_{n}")
                bba4 = bass.AP(bbs, bsl * TC, [[2 * TC, 128], [0, 4], [1, TC]])
                tt("vector", OP.mult,
                   bin_b[:, :].rearrange("p (d t) -> p d t", d=4),
                   du_g[:, :].rearrange("p (d t) -> p d t", d=4), bba4,
                   [dug_b[0], dug_b[1], dug_b[2], dug_b[3], bbs_b[bsl]],
                   [bin_bb[0], bin_bb[1], bin_bb[2], bin_bb[3]], tag=f"bi{g}_{n}")
                yield
                for dt in range(4):
                    idx = (g * 4 + dt) * DS + n
                    dsl = dt + 4 * (n % 2)
                    scan(hseg_b[:, dt * TC:(dt + 1) * TC],
                         da_b[:, dsl * TC:(dsl + 1) * TC],
                         bin_b[:, dt * TC:(dt + 1) * TC], h_state[:, idx:idx + 1],
                         [da_bb[dsl], bin_bb[dt], hst_b[idx]], [hs_bb[dt]],
                         tag=f"sc{g}_{n}")
                yield
                for dt in range(4):
                    idx = (g * 4 + dt) * DS + n
                    cp("vector", h_state[:, idx:idx + 1],
                       hseg_b[:, (dt + 1) * TC - 1:(dt + 1) * TC],
                       [hs_bb[dt]], [hst_b[idx]], tag=f"cr{g}_{n}")
                cba4 = bass.AP(cbs, bsl * TC, [[2 * TC, 128], [0, 4], [1, TC]])
                tt("vector", OP.mult,
                   hp_b[:, :].rearrange("p (d t) -> p d t", d=4),
                   hseg_b[:, :].rearrange("p (d t) -> p d t", d=4), cba4,
                   [hs_bb[0], hs_bb[1], hs_bb[2], hs_bb[3], cbs_b[bsl]],
                   [hp_bb[0], hp_bb[1], hp_bb[2], hp_bb[3]], tag=f"hp{g}_{n}")
                yield
                for dt in range(4):
                    yga = y_g[:, dt * TC:(dt + 1) * TC]
                    hpa = hp_b[:, dt * TC:(dt + 1) * TC]
                    if n == 0:
                        cp("gpsimd", yga, hpa, [hp_bb[dt]], [yg_b[dt]], tag=f"y0{g}")
                    else:
                        tt("gpsimd", OP.add, yga, yga, hpa,
                           [hp_bb[dt], yg_b[dt]], [yg_b[dt]], tag=f"ya{g}_{n}")
                yield
            slv, slb, slsem = wslot()
            dma("sync", slv[:, 0:KM * 512].rearrange("p (k c) -> p k c", k=KM),
                s_win[:, DI + g * 512: DI + (g + 1) * 512].rearrange("(k p) c -> p k c", p=128),
                win_regs, [slb], slsem, tag=f"lwres{g}")
            for dt in range(4):
                m = g * 4 + dt
                pm, pb = mx_ps()
                for k in range(KM):
                    mm(pm[:, :], slv[:, k * 512 + dt * 128: k * 512 + (dt + 1) * 128],
                       xnt[:, k * TC:(k + 1) * TC], k == 0, k == KM - 1,
                       [slb, xnt_b], pb, tag=f"res{m}")
                sra = sres_t[:, (dt % 2) * TC:(dt % 2 + 1) * TC]
                act(sra, pm[:, :], AF.Silu, [pb], [srt_b[dt % 2]], tag=f"sres{m}")
                sua = t_su[:, m * TC:(m + 1) * TC]
                stt(gat_t[:, :], sua, d_colv[:, m:m + 1], y_g[:, dt * TC:(dt + 1) * TC],
                    OP.mult, OP.add, [tsu_b[m], buf("dcol"), yg_b[dt]], [gat_b],
                    tag=f"g1{m}")
                tt("vector", OP.mult, sua, gat_t[:, :], sra,
                   [gat_b, srt_b[dt % 2]], [tsu_b[m]], tag=f"g2{m}")
                yield
        for cg in range(2):
            wo_sl = []
            for half in range(2):
                slv, slb, slsem = wslot()
                dma("sync", slv[:, 0:8 * 512].rearrange("p (k c) -> p k c", k=8),
                    s_wout[half * 1024:(half + 1) * 1024, cg * 512:(cg + 1) * 512]
                    .rearrange("(k p) c -> p k c", p=128),
                    wout_regs, [slb], slsem, tag=f"lwo{cg}{half}")
                wo_sl.append((slv, slb))
            for ml in range(4):
                m = cg * 4 + ml
                pm, pb = mx_ps()
                for k in range(KI):
                    slv, slb = wo_sl[k // 8]
                    kk = k % 8
                    mm(pm[:, :], slv[:, kk * 512 + ml * 128: kk * 512 + (ml + 1) * 128],
                       t_su[:, k * TC:(k + 1) * TC], k == 0, k == KI - 1,
                       [slb, tsu_b[k]], pb, tag=f"wo{m}")
                act(ymix_t[:, m * TC:(m + 1) * TC], pm[:, :], AF.Copy,
                    [pb], [ymix_b[m]], tag=f"ewo{m}")
                yield
        for j in range(4):
            xr = mx_roll[:, (j % 2) * 1024:(j % 2 + 1) * 1024]
            xrb = mxr_b[j % 2]
            dma("sync", xr, d_x[b, t0 + j * 128: t0 + (j + 1) * 128, :],
                [xrb], [xrb], "ld_x", tag=f"rlx{b}{c}{j}")
            for i in range(KM):
                tp(p_t16[:, i * 128:(i + 1) * 128],
                   ymix_t[:, i * TC + j * 128: i * TC + (j + 1) * 128],
                   ident_b[:, :], [ymix_b[i], idb_b], [pt16_b], tag=f"tym{j}{i}")
            tt("vector", OP.add, xr, xr, p_t16[:, :],
               [xrb, pt16_b], [xrb], tag=f"res{j}")
            dma("scalar", s_x1[b, t0 + j * 128: t0 + (j + 1) * 128, :], xr,
                [xrb], [buf(("x1s", b, c))], "st_x1", tag=f"sx1{b}{c}{j}")
            xn_ap = x1n_t[:, 0:1024]
            layer_norm(xr, [xrb], xn_ap, x1n_b, f"l2_{b}{c}{j}")
            for i in range(KM):
                tp(p_t16[:, i * 128:(i + 1) * 128], xn_ap[:, i * 128:(i + 1) * 128],
                   ident_b[:, :], [x1n_b, idb_b], [pt16_b], tag=f"tx1n{j}_{i}")
            cp("vector", x1nt3[:, :, j * 128:(j + 1) * 128], pt16_3,
               [pt16_b], [x1nt_b], tag=f"ex1n{j}")
            yield
        dma("scalar", s_x1nt[b].rearrange("(k p) t -> p k t", p=128)[:, :, t0:t0 + TC],
            x1nt3, [x1nt_b], [buf(("x1nts", b, c))], "st_x1nt", tag=f"sx1n{b}{c}")

    def ffn_group(b, q):
        t0 = q * TC
        dma("sync", t_b2[:, :].rearrange("p (k t) -> p k t", k=KM),
            s_x1nt[b].rearrange("(k p) t -> p k t", p=128)[:, :, t0:t0 + TC],
            [buf(("x1nts", b, q))] + tb2_b, tb2_b, "ld_b2", tag=f"lfx{b}{q}")
        for tt_ in range(4):
            pm, pb = ff_ps()
            for k in range(KM):
                mm(pm[:, 0:NE], t_b2[:, k * TC + tt_ * 128: k * TC + (tt_ + 1) * 128],
                   wr_sb[:, k * NE:(k + 1) * NE], k == 0, k == KM - 1,
                   [tb2_b[k], buf("wrT")], pb, tag=f"sco{tt_}")
            sca = sc_sb[:, tt_ * NE:(tt_ + 1) * NE]
            tt("vector", OP.add, sca, pm[:, 0:NE], br_bc[:, :],
               [pb, buf("brbc")], [sc_b], tag=f"sce{tt_}")
            red("max", tk1[:, 0:1], sca, [sc_b], [tk1_b], tag="smx")
            ts("vector", tk1[:, 7:8], tk1[:, 0:1], -1.0, None, OP.mult, None,
               [tk1_b], [tk1_b], tag="snm")
            act(tk_sb[:, 0:8], sca, AF.Exp, [sc_b, tk1_b], [tk_b],
                bias=tk1[:, 7:8], tag="sex")
            red("sum", tk1[:, 1:2], tk_sb[:, 0:8], [tk_b], [tk1_b], tag="ssm")
            E("vector", lambda En: En.reciprocal(tk1[:, 2:3], tk1[:, 1:2]),
              [tk1_b], [tk1_b], tag="srp")
            ts("vector", tk_sb[:, 0:8], tk_sb[:, 0:8], tk1[:, 2:3], None, OP.mult, None,
               [tk_b, tk1_b], [tk_b], tag="sw")
            tt("vector", OP.add, tk_sb[:, 8:16], sca, bn_bc[:, :],
               [sc_b, buf("bnbc")], [tk_b], tag="sp1")
            red("max", tk1[:, 3:4], tk_sb[:, 8:16], [tk_b], [tk1_b], tag="sm1")
            ts("vector", tk_sb[:, 32:40], tk_sb[:, 8:16], tk1[:, 3:4], None, OP.is_ge, None,
               [tk_b, tk1_b], [tk_b], tag="sk1")
            stt(tk_sb[:, 16:24], tk_sb[:, 32:40], -1e30, tk_sb[:, 8:16],
                OP.mult, OP.add, [tk_b], [tk_b], tag="sms")
            red("max", tk1[:, 4:5], tk_sb[:, 16:24], [tk_b], [tk1_b], tag="sm2")
            ts("vector", tk_sb[:, 40:48], tk_sb[:, 16:24], tk1[:, 4:5], None, OP.is_ge, None,
               [tk_b, tk1_b], [tk_b], tag="sk2")
            tt("vector", OP.add, tk_sb[:, 32:40], tk_sb[:, 32:40], tk_sb[:, 40:48],
               [tk_b], [tk_b], tag="skm")
            tt("vector", OP.mult, tk_sb[:, 24:32], tk_sb[:, 0:8], tk_sb[:, 32:40],
               [tk_b], [tk_b], tag="svl")
            red("sum", tk1[:, 5:6], tk_sb[:, 24:32], [tk_b], [tk1_b], tag="sdn")
            E("vector", lambda En: En.reciprocal(tk1[:, 6:7], tk1[:, 5:6]),
              [tk1_b], [tk1_b], tag="sdr")
            ts("vector", ew_sb[:, tt_ * NE:(tt_ + 1) * NE], tk_sb[:, 24:32],
               tk1[:, 6:7], None, OP.mult, None, [tk_b, tk1_b], [ew_b], tag=f"sew{tt_}")
            yield
            pm2, pb2 = ff_ps()
            tp(pm2[0:8, 0:128], ew_sb[:, tt_ * NE:(tt_ + 1) * NE], ident_f[:, :],
               [ew_b, idf_b], [pb2], tag=f"ewt{tt_}")
            cp("vector", ewt_sb[0:8, tt_ * 128:(tt_ + 1) * 128], pm2[0:8, 0:128],
               [pb2], [ewt_b], tag=f"ewte{tt_}")
            yield
        for e in range(NE):
            mm(p_mm[0][:, :], onehot[:, e * 128:(e + 1) * 128], ewt_sb[:, :],
               True, True, [oh_b, ewt_b], pmm_b[0], tag=f"ewb{e}")
            act(ewb_sb[:, :], p_mm[0][:, :], AF.Copy, [pmm_b[0]], [ewb_b], tag=f"ewbe{e}")
            yield
            for band in range(4):
                slvA, slbA, slsemA = wslot_f()
                dma("sync", slvA[:, 0:KM * 512].rearrange("p (k c) -> p k c", k=KM),
                    s_wl1[e][:, band * 512:(band + 1) * 512]
                    .rearrange("(k p) c -> p k c", p=128),
                    wl1_regs[e], [slbA], slsemA, tag=f"lA{e}{band}")
                slvB, slbB, slsemB = wslot_f()
                dma("sync", slvB[:, 0:KM * 512].rearrange("p (k c) -> p k c", k=KM),
                    s_wl1[e][:, H + band * 512: H + (band + 1) * 512]
                    .rearrange("(k p) c -> p k c", p=128),
                    wl1_regs[e], [slbB], slsemB, tag=f"lB{e}{band}")
                for ml in range(4):
                    m = band * 4 + ml
                    pa, pab = p_mm[m % 2], pmm_b[m % 2]
                    pbv, pbb2 = p_ax[m % 2], pax_b[m % 2]
                    for k in range(KM):
                        mm(pa[:, :], slvA[:, k * 512 + ml * 128: k * 512 + (ml + 1) * 128],
                           t_b2[:, k * TC:(k + 1) * TC], k == 0, k == KM - 1,
                           [slbA, tb2_b[k]], pab, tag=f"h1a{e}{m}")
                    for k in range(KM):
                        mm(pbv[:, :], slvB[:, k * 512 + ml * 128: k * 512 + (ml + 1) * 128],
                           t_b2[:, k * TC:(k + 1) * TC], k == 0, k == KM - 1,
                           [slbB, tb2_b[k]], pbb2, tag=f"h1b{e}{m}")
                    sl = m % 2
                    saa = sa_b[:, sl * TC:(sl + 1) * TC]
                    act(saa, pa[:, :], AF.Silu, [pab, buf("bl1")],
                        [sab_b[sl]], bias=bl1_col[:, e * MH + m: e * MH + m + 1],
                        tag=f"sl{e}{m}")
                    stt(t_g[:, m * TC:(m + 1) * TC], pbv[:, :],
                        bl1_col[:, e * MH + 16 + m: e * MH + 17 + m], saa,
                        OP.add, OP.mult, [pbb2, buf("bl1"), sab_b[sl]],
                        [tg_b[m]], tag=f"gl{e}{m}")
                    yield
            for mp in range(4):
                slv, slb, slsem = wslot_f()
                dma("sync", slv[:, 0:KI * 256].rearrange("p (k c) -> p k c", k=KI),
                    s_wl2[e][:, mp * 256:(mp + 1) * 256]
                    .rearrange("(k p) c -> p k c", p=128),
                    wl2_regs[e], [slb], slsem, tag=f"lW2{e}{mp}")
                for m2l in range(2):
                    m2 = mp * 2 + m2l
                    pm, pb = p_mm[m2 % 2], pmm_b[m2 % 2]
                    for k in range(KI):
                        mm(pm[:, :], slv[:, k * 256 + m2l * 128: k * 256 + (m2l + 1) * 128],
                           t_g[:, k * TC:(k + 1) * TC], k == 0, k == KI - 1,
                           [slb, tg_b[k]], pb, tag=f"w2{e}{m2}")
                    ya = y_acct[:, m2 * TC:(m2 + 1) * TC]
                    sl = m2 % 2
                    if e == 0:
                        stt(ya, pm[:, :], bl2_col[:, e * KM + m2: e * KM + m2 + 1],
                            ewb_sb[:, :], OP.add, OP.mult,
                            [pb, buf("bl2"), ewb_b], [yat_b[m2]], tag=f"yi{e}{m2}")
                    else:
                        tmpa = tmp_s[:, sl * TC:(sl + 1) * TC]
                        stt(tmpa, pm[:, :], bl2_col[:, e * KM + m2: e * KM + m2 + 1],
                            ewb_sb[:, :], OP.add, OP.mult,
                            [pb, buf("bl2"), ewb_b], [tmp_b[sl]], tag=f"ym{e}{m2}")
                        tt("gpsimd", OP.add, ya, ya, tmpa,
                           [tmp_b[sl], yat_b[m2]], [yat_b[m2]], tag=f"yA{e}{m2}")
                    yield
        for j in range(4):
            xr = fx_roll[:, (j % 2) * 1024:(j % 2 + 1) * 1024]
            xrb = fxr_b[j % 2]
            dma("sync", xr, s_x1[b, t0 + j * 128: t0 + (j + 1) * 128, :],
                [buf(("x1s", b, q)), xrb], [xrb], "ld_b2", tag=f"lX1{b}{q}{j}")
            for half in range(2):
                pm, pb = ff_ps()
                for i in range(4):
                    mdm = half * 4 + i
                    tp(pm[:, i * 128:(i + 1) * 128],
                       y_acct[:, mdm * TC + j * 128: mdm * TC + (j + 1) * 128],
                       ident_f[:, :], [yat_b[mdm], idf_b], [pb], tag=f"tya{j}{half}")
                xh = xr[:, half * 512:(half + 1) * 512]
                tt("vector", OP.add, xh, xh, pm[:, :],
                   [xrb, pb], [xrb], tag=f"fo{j}{half}")
            yb = buf(("ys", b, q, j))
            dma("scalar", d_y[b, t0 + j * 128: t0 + (j + 1) * 128, :], xr,
                [xrb], [yb], "st_y", tag=f"sy{b}{q}{j}")
            ystore_bufs.append(yb)
            yield

    pend = []
    ffg = [None]

    def ffn_step(k):
        for _ in range(k):
            if ffg[0] is None:
                if not pend:
                    return
                ffg[0] = ffn_group(*pend.pop(0))
            try:
                next(ffg[0])
            except StopIteration:
                ffg[0] = None

    for b in range(BPC):
        mset(h_state[:, :], 0.0, hst_b, tag=f"hz{b}")
        for c in range(NCH):
            mg = mixer_chunk(b, c)
            INTERLEAVE = 1
            alive = True
            while alive:
                for _ in range(5):
                    try:
                        next(mg)
                    except StopIteration:
                        alive = False
                        break
                if INTERLEAVE:
                    ffn_step(2)
            if not INTERLEAVE:
                ffn_step(10 ** 6)
            pend.append((b, c))
    while pend or ffg[0] is not None:
        ffn_step(8)

    E("sync", lambda En: En.nop(), ystore_bufs, [], tag="final_wait")

    streams, tok, dcnt, ecnt = S.build_streams()
    dma_sem_map = {k: sems[k] for k in sems if k.startswith(("ld_", "st_"))}
    eng_sem_map = {e: sems[k] for e, k in
                   (("tensor", "pe"), ("vector", "v"), ("scalar", "s"),
                    ("gpsimd", "g"), ("sync", "pe"))}

    with es:
        with nc.Block() as blk0:
            @blk0.sync
            def _(sy):
                for s_ in sems.values():
                    sy.sem_clear(s_)

        with nc.Block() as blk:
            def run_stream(eng_name):
                def body(En):
                    ops = S.ops
                    for cmd in streams[eng_name]:
                        if cmd[0] == "wait":
                            (kind, key), val = cmd[1], cmd[2]
                            sem = dma_sem_map[key] if kind == "d" else eng_sem_map[key]
                            En.wait_ge(sem, val)
                        elif cmd[0] == "drain":
                            En.drain()
                        else:
                            i = cmd[1]
                            inst = ops[i][2](En)
                            tk = tok[i]
                            if tk is not None:
                                if tk[0] == "d":
                                    inst.then_inc(dma_sem_map[tk[1]], 16)
                                else:
                                    inst.then_inc(eng_sem_map[tk[1]], 1)
                return body

            blk.sync(run_stream("sync"))
            blk.tensor(run_stream("tensor"))
            blk.vector(run_stream("vector"))
            blk.scalar(run_stream("scalar"))
            blk.gpsimd(run_stream("gpsimd"))

    return nc


_CACHE = {}


def kernel(**inputs):
    if "nc" not in _CACHE:
        _CACHE["nc"] = build_nc()
    nc = _CACHE["nc"]
    key_map = {
        "x": "L0_attn_x", "w_in": "L0_attn_W_in", "w_delta": "L0_attn_W_delta",
        "b_delta": "L0_attn_b_delta", "w_b": "L0_attn_W_B", "w_c": "L0_attn_W_C",
        "a_log": "L0_attn_A_log", "d_param": "L0_attn_D_param",
        "w_out": "L0_attn_W_out", "wr": "L0_ffn_Wr", "br": "L0_ffn_br",
        "wl1": "L0_ffn_Wl1", "bl1": "L0_ffn_bl1", "wl2": "L0_ffn_Wl2",
        "bl2": "L0_ffn_bl2", "biases_n": "L0_ffn_biases_N",
    }
    full = {k: np.ascontiguousarray(np.asarray(inputs[v], dtype=np.float32))
            for k, v in key_map.items()}
    in_maps = []
    for c in range(NCORES):
        m = dict(full)
        m["x"] = np.ascontiguousarray(full["x"][c * BPC:(c + 1) * BPC])
        in_maps.append(m)
    res = run_bass_kernel_spmd(nc, in_maps, core_ids=list(range(NCORES)))
    out = np.concatenate([np.asarray(res.results[i]["y"]) for i in range(NCORES)], axis=0)
    return out.astype(np.float32, copy=False)
